# version 01
# baseline (speedup 1.0000x reference)
import contextlib
import numpy as np
import concourse.bass as bass
import concourse.mybir as mybir
from concourse.bass_utils import run_bass_kernel_spmd

F32 = mybir.dt.float32
BF16 = mybir.dt.bfloat16
AF = mybir.ActivationFunctionType
ALU = mybir.AluOpType
AX = mybir.AxisListType

D = 2048
T = 2048
NCH = 16
H = 16
NB = 2
NCORE = 8
EPS = 1e-5

ENGINES = ("tensor", "vector", "scalar", "gpsimd", "sync")

CH_XC = 0
CH_W0, CH_W1 = 16, 17
CH_SMALL = 18
CH_EV = 19
CH_XW = 27
CH_R0, CH_R1, CH_R2, CH_R3, CH_R4, CH_R5 = 43, 44, 45, 46, 47, 48
CH_XO0, CH_XO1 = 49, 50
CH_STAT = 51
N_DMA_CH = 52


def tmax(*toks):
    best = {}
    for t in toks:
        if t is None:
            continue
        if isinstance(t, list):
            for k, v in t:
                if best.get(k, -1) < v:
                    best[k] = v
            continue
        k, v = t
        if best.get(k, -1) < v:
            best[k] = v
    return [(k, v) for k, v in best.items()]


class Prog:
    def __init__(self, nc, es, n_dma_chans):
        self.nc = nc
        self.streams = {e: [] for e in ENGINES}
        self.counts = {e: 0 for e in ENGINES}
        self.sems = {e: es.enter_context(nc.semaphore(f"sem_{e}")) for e in ENGINES}
        self.dma_sems = [es.enter_context(nc.semaphore(f"dsem{i}")) for i in range(n_dma_chans)]
        self.dma_counts = [0] * n_dma_chans
        self.warnings = []

    def _check(self, deps):
        for key, val in deps:
            if isinstance(key, tuple) and key[0] == "dma":
                if val != self.dma_counts[key[1]]:
                    self.warnings.append(f"stale dma wait chan {key[1]}: {val} != {self.dma_counts[key[1]]}")

    def op(self, eng, fn, deps=(), inc=True, selfwait=False):
        deps = tmax(*deps) if deps else ()
        if deps:
            self._check(deps)
        tok = None
        if inc:
            self.counts[eng] += 1
            tok = (eng, self.counts[eng])
        self.streams[eng].append((fn, deps, eng if inc else None, selfwait))
        return tok

    def dma(self, queue, chan, fn, deps=()):
        deps = tmax(*deps) if deps else ()
        if deps:
            self._check(deps)
        self.dma_counts[chan] += 16
        tok = (("dma", chan), self.dma_counts[chan])
        self.streams[queue].append((fn, deps, ("dma", chan), False))
        return tok

    def _sem_of(self, key):
        if isinstance(key, tuple) and key[0] == "dma":
            return self.dma_sems[key[1]]
        return self.sems[key]

    def build(self, block):
        for eng_name in ENGINES:
            stream = self.streams[eng_name]
            deco = getattr(block, eng_name)

            def make_body(eng_name, stream):
                def body(e):
                    satisfied = {}
                    for fn, deps, inc_key, selfwait in stream:
                        for key, val in deps:
                            if key == eng_name:
                                if not selfwait:
                                    continue
                            if satisfied.get(key, -1) >= val:
                                continue
                            e.wait_ge(self._sem_of(key), val)
                            satisfied[key] = val
                        instr = fn(e)
                        if inc_key is not None and instr is not None:
                            if isinstance(inc_key, tuple):
                                instr.then_inc(self._sem_of(inc_key), 16)
                            else:
                                instr.then_inc(self.sems[inc_key], 1)
                return body

            deco(make_body(eng_name, stream))


class KB:
    def __init__(self, n_stages=4):
        self.n_stages = n_stages
        nc = bass.Bass("TRN2", target_bir_lowering=False, debug=False)
        self.nc = nc
        self.es = contextlib.ExitStack()
        dt = nc.dram_tensor

        self.xin = dt("xin", [NB, D, T], F32, kind="ExternalInput").ap()
        self.wq = dt("wq", [D, D], F32, kind="ExternalInput").ap()
        self.wk = dt("wk", [D, D], F32, kind="ExternalInput").ap()
        self.wv = dt("wv", [D, D], F32, kind="ExternalInput").ap()
        self.wo = dt("wo", [D, D], F32, kind="ExternalInput").ap()
        self.wqkv = dt("wqkv", [D, 3 * D], F32, kind="ExternalInput").ap()
        self.wout = dt("wout", [D, D], F32, kind="ExternalInput").ap()
        self.wft = dt("wft", [D, D], F32, kind="ExternalInput").ap()
        self.rwt = dt("rwt", [128, 128], F32, kind="ExternalInput").ap()
        self.atw = dt("atw", [128, NCH * 128], F32, kind="ExternalInput").ap()
        self.btw = dt("btw", [128, D], F32, kind="ExternalInput").ap()
        self.slopes = dt("slopes", [1, H], F32, kind="ExternalInput").ap()
        self.pb = {}
        for name in ("bq", "bk", "bo", "bqkv_q", "bqkv_k", "bout"):
            self.pb[name] = dt(f"pb_{name}", [128, NCH], F32, kind="ExternalInput").ap()
        self.row_bv = dt("row_bv", [1, D], F32, kind="ExternalInput").ap()
        self.row_bqv = dt("row_bqv", [1, D], F32, kind="ExternalInput").ap()
        self.c_eye_b = dt("c_eye_b", [128, 128], BF16, kind="ExternalInput").ap()
        self.c_eye_f = dt("c_eye_f", [128, 128], F32, kind="ExternalInput").ap()
        self.c_dif = dt("c_dif", [128, 128], F32, kind="ExternalInput").ap()
        self.c_mask = dt("c_mask", [128, 128], F32, kind="ExternalInput").ap()
        self.c_rowidx = dt("c_rowidx", [128, 128], F32, kind="ExternalInput").ap()
        self.c_col128m = dt("c_col128m", [128, 1], F32, kind="ExternalInput").ap()
        self.c_onescol = dt("c_onescol", [128, 1], BF16, kind="ExternalInput").ap()
        self.c_allones = dt("c_allones", [128, 128], BF16, kind="ExternalInput").ap()

        self.y = dt("y", [NB, D, T], F32, kind="ExternalOutput").ap()
        self.xs = [dt(f"xs{i}", [D, T], F32, kind="Internal").ap() for i in range(2)]
        self.a_q = dt("a_q", [H, 128, T], BF16, kind="Internal").ap()
        self.a_k = dt("a_k", [H, 128, T], BF16, kind="Internal").ap()
        self.a_v = dt("a_v", [H, 128, NCH, 128], BF16, kind="Internal").ap()
        self.b_q = dt("b_q", [NCH, 128, H, 128], BF16, kind="Internal").ap()
        self.b_k = dt("b_k", [NCH, 128, H, 128], BF16, kind="Internal").ap()
        self.b_v = dt("b_v", [NCH, 128, D], BF16, kind="Internal").ap()
        self.scr_stat = dt("scr_stat", [1, 2 * T], F32, kind="Internal").ap()
        self.scr_w8 = dt("scr_w8", [8, T], F32, kind="Internal").ap()

        es = self.es
        sb = lambda name, shape, dtyp: es.enter_context(nc.sbuf_tensor(name, shape, dtyp))
        self.XB = sb("XB", [128, NCH * T], BF16)
        self.XBF = self.XB.bitcast(F32)
        self.AR = sb("AR", [128, 18432], BF16)
        self.ARF = self.AR.bitcast(F32)
        self.WCB = [sb(f"WCB{i}", [128, NCH * 512], BF16) for i in range(2)]
        self.STV = sb("STV", [1, 3 * T], F32)
        self.EYEB = sb("EYEB", [128, 128], BF16)
        self.EYEF = sb("EYEF", [128, 128], F32)
        self.DIF = sb("DIF", [128, 128], F32)
        self.MASK = sb("MASK", [128, 128], F32)
        self.ROWIDX = sb("ROWIDX", [128, 128], F32)
        self.COLM = sb("COLM", [128, 1], F32)
        self.ONESC = sb("ONESC", [128, 1], BF16)
        self.ALLONES = sb("ALLONES", [128, 128], BF16)
        self.SLP = sb("SLP", [128, 64], F32)
        self.INTRA = sb("INTRA", [128, H * 128], BF16)
        self.QDECF = sb("QDECF", [128, H * 128], BF16)
        self.RWT = sb("RWT", [128, 128], BF16)
        self.ATW = sb("ATW", [128, NCH * 128], BF16)
        self.BTW = sb("BTW", [128, D], BF16)
        self.PBS = {name: sb(f"PBS_{name}", [128, NCH], F32) for name in self.pb}
        self.WTB = sb("WTB", [8, T], F32)
        self.W8S = sb("W8S", [128, 40], F32)

        self.PD = [es.enter_context(nc.psum_tensor(f"PD{i}", [128, 1024], F32)) for i in range(4)]
        self.PDB = self.PD[3].bitcast(BF16)
        self.bank_state = [[] for _ in range(8)]

        self.P = Prog(nc, es, N_DMA_CH)

        self.xb_consumed = None
        self.next_x_ready = [None] * NCH
        self.buf_gate = {}
        self.wcb_busy = [None, None]
        self.evb_busy = [None] * 8
        self.scr_gate = {}
        self.taps_on = False
        self.tap_toks = []
        self._tapn = 0
        self._tapnames = set()

    def bank(self, i):
        return self.PD[i // 2][:, (i % 2) * 512:(i % 2 + 1) * 512]

    def bank_row0(self, i):
        return self.PD[i // 2][0:1, (i % 2) * 512:(i % 2 + 1) * 512]

    def acq(self, banks, *extra):
        deps = []
        for b in banks:
            deps += self.bank_state[b]
        for e in extra:
            if e is None:
                continue
            deps += e if isinstance(e, list) else [e]
        return deps

    def rel(self, banks, toks):
        toks = toks if isinstance(toks, list) else [toks]
        for b in banks:
            self.bank_state[b] = [t for t in toks if t is not None]

    def arb(self, off, cols):
        return self.AR[:, off:off + cols]

    def arf(self, off, cols):
        return self.ARF[:, off // 2:off // 2 + cols]

    def load_wcb(self, wsrc, col0, wblk, slot, extra=()):
        deps = list(extra)
        if self.wcb_busy[slot] is not None:
            deps.append(self.wcb_busy[slot])
        dst = self.WCB[slot][:, 0:NCH * wblk]

        def fn(g, wsrc=wsrc, col0=col0, wblk=wblk, dst=dst):
            return g.dma_start(out=dst.rearrange("p (k f) -> p k f", f=wblk),
                               in_=wsrc[:, col0:col0 + wblk].rearrange("(k p) f -> p k f", p=128))
        return self.P.dma("gpsimd", CH_W0 + slot, fn, deps=deps)

    def evb(self, slot):
        return self.arb(slot * 512, 512)

    def tap(self, name, ap, deps, dtype=None):
        if not self.taps_on:
            return
        shape = list(ap.shape)
        dtyp = dtype or ap.dtype
        dten = self.nc.dram_tensor(f"tap_{name}_{self._tapn}" if name in self._tapnames else f"tap_{name}",
                                   shape, dtyp, kind="ExternalOutput").ap()
        self._tapnames.add(name)
        self._tapn += 1
        tok = self.P.dma("sync", CH_STAT, lambda s, dten=dten, ap=ap: s.dma_start(out=dten[:, :], in_=ap),
                         deps=deps + (self.scr_gate.get("scr_stat") or []))
        self.tap_toks.append(tok)
        self.scr_gate.setdefault("taps", []).append(tok)
        return tok

    def emit_all(self):
        self.load_consts()
        self.final_writes = []
        stages = ["a", "f", "b", "f"][: self.n_stages]
        for b in range(NB):
            xsrc = self.xin[b]
            self.next_x_ready = [None] * NCH
            for si, kind in enumerate(stages):
                last = si == len(stages) - 1
                xdst = self.y[b] if last else self.xs[si % 2]
                if kind == "a":
                    self.emit_attn_a(xsrc, xdst)
                elif kind == "f":
                    self.emit_ffn(xsrc, xdst)
                else:
                    self.emit_attn_b(xsrc, xdst)
                xsrc = xdst
        self.P.op("sync", lambda s: s.nop(), deps=[tmax(*self.final_writes, *self.tap_toks)], inc=False)

    def load_consts(self):
        P = self.P
        pairs = [
            (self.EYEB, self.c_eye_b), (self.EYEF, self.c_eye_f), (self.DIF, self.c_dif),
            (self.MASK, self.c_mask), (self.ROWIDX, self.c_rowidx), (self.COLM, self.c_col128m),
            (self.ONESC, self.c_onescol), (self.ALLONES, self.c_allones),
            (self.RWT, self.rwt), (self.ATW, self.atw), (self.BTW, self.btw),
        ]
        t_all = None
        for dst, src in pairs:
            t_all = P.dma("gpsimd", CH_SMALL, lambda g, dst=dst, src=src: g.dma_start(out=dst[:], in_=src[:, :]))
        for name in self.pb:
            t_all = P.dma("gpsimd", CH_SMALL,
                          lambda g, d=self.PBS[name], s=self.pb[name]: g.dma_start(out=d[:], in_=s[:, :]))
        t_all = P.dma("gpsimd", CH_SMALL,
                      lambda g: g.dma_start(out=self.SLP[:, 0:16], in_=self.slopes.broadcast_to([128, H])))
        t_neg = P.op("vector", lambda v: v.tensor_scalar_mul(self.SLP[:, 16:32], self.SLP[:, 0:16], -1.0),
                     deps=[t_all])
        t_bd = P.op("scalar", lambda s: s.activation(self.SLP[:, 32:48], self.SLP[:, 0:16], AF.Exp, scale=-128.0),
                    deps=[t_all])
        out_toks = [t_neg, t_bd]
        for h in range(H):
            out_toks.append(P.op("scalar", lambda s, h=h: s.activation(
                self.SLP[:, 48 + h:49 + h], self.COLM[:, 0:1], AF.Exp,
                scale=self.SLP[:, 16 + h:17 + h]), deps=[t_neg, t_all]))
        tmp = self.arf(0, 128)
        prev = None
        for h in range(H):
            t1 = P.op("scalar", lambda s, h=h: s.activation(
                tmp, self.DIF[:, :], AF.Exp, scale=self.SLP[:, 0 + h:1 + h]),
                deps=[t_all] + ([prev] if prev else []))
            t2 = P.op("vector", lambda v, h=h: v.tensor_mul(
                self.INTRA[:, h * 128:(h + 1) * 128], tmp, self.MASK[:, :]),
                deps=[t1, t_all])
            prev = t2
            out_toks.append(t2)
        for h in range(H):
            out_toks.append(P.op("scalar", lambda s, h=h: s.activation(
                self.QDECF[:, h * 128:(h + 1) * 128], self.ROWIDX[:, :], AF.Exp,
                scale=self.SLP[:, 16 + h:17 + h]), deps=[t_neg, t_all]))
        self.const_toks = tmax(t_all, *out_toks)
        self.xb_consumed = tmax(*out_toks)

    def emit_stats_cast(self, xsrc):
        P = self.P
        xc = []
        for c in range(NCH):
            deps = [self.xb_consumed, self.next_x_ready[c]]
            xc.append(P.dma("gpsimd", CH_XC + c, lambda g, c=c, xsrc=xsrc: g.dma_start(
                out=self.XB[:, c * T:(c + 1) * T], in_=xsrc[c * 128:(c + 1) * 128, :]), deps=deps))
        self.xsrc_reads = list(xc)
        deps0 = self.acq([4, 5, 6, 7], [self.const_toks])
        last1 = {}
        for c in range(NCH):
            for tb in range(4):
                def mm(t, c=c, tb=tb):
                    return t.matmul(self.bank_row0(4 + tb), self.ONESC[:, 0:1],
                                    self.XB[:, c * T + tb * 512: c * T + tb * 512 + 512],
                                    start=(c == 0), stop=(c == NCH - 1))
                last1[tb] = P.op("tensor", mm, deps=deps0 + [xc[c]], inc=(c == NCH - 1))
        mtok = []
        for tb in range(4):
            ev = P.op("vector", lambda v, tb=tb: v.tensor_scalar_mul(
                self.STV[0:1, tb * 512:(tb + 1) * 512], self.bank_row0(4 + tb), 1.0 / D), deps=[last1[tb]])
            mtok.append(ev)
            self.rel([4 + tb], [ev])
        deps2 = self.acq([4, 5, 6, 7])
        sq_free = [None, None]
        last2 = {}
        for c in range(NCH):
            slot = c % 2
            sq = self.arb(slot * 2048, 2048)
            sdeps = [xc[c], self.xb_consumed]
            if sq_free[slot] is not None:
                sdeps.append(sq_free[slot])
            tsq = P.op("scalar", lambda s, c=c, sq=sq: s.activation(
                sq, self.XB[:, c * T:(c + 1) * T], AF.Square), deps=sdeps)
            for tb in range(4):
                def mm(t, sq=sq, tb=tb, c=c):
                    return t.matmul(self.bank_row0(4 + tb), self.ONESC[:, 0:1],
                                    sq[:, tb * 512: tb * 512 + 512],
                                    start=(c == 0), stop=(c == NCH - 1))
                last2[tb] = P.op("tensor", mm, deps=deps2 + [tsq], inc=(c == NCH - 1 or tb == 3))
            sq_free[slot] = last2[3]
        qtok = []
        for tb in range(4):
            ev = P.op("vector", lambda v, tb=tb: v.tensor_scalar_mul(
                self.STV[0:1, T + tb * 512: T + (tb + 1) * 512], self.bank_row0(4 + tb), 1.0 / D),
                deps=[last2[tb]])
            qtok.append(ev)
            self.rel([4 + tb], [ev])
        self.sq_mm_done = qtok
        ftok, ntok = [], []
        for tb in range(4):
            m = self.STV[0:1, tb * 512:(tb + 1) * 512]
            q = self.STV[0:1, T + tb * 512: T + (tb + 1) * 512]
            iv = self.STV[0:1, 2 * T + tb * 512: 2 * T + (tb + 1) * 512]
            t1 = P.op("vector", lambda v, m=m, iv=iv: v.tensor_mul(iv, m, m), deps=[mtok[tb], qtok[tb]])
            t2 = P.op("vector", lambda v, q=q, iv=iv: v.tensor_sub(q, q, iv), deps=[t1])
            t3 = P.op("vector", lambda v, q=q: v.tensor_scalar_add(q, q, EPS), deps=[t2])
            t4 = P.op("vector", lambda v, q=q, iv=iv: v.reciprocal(iv, q), deps=[t3])
            t5 = P.op("scalar", lambda s, iv=iv: s.activation(iv, iv, AF.Sqrt), deps=[t4])
            tn = P.op("vector", lambda v, m=m: v.tensor_scalar_mul(m, m, -1.0), deps=[t1])
            ftok.append(t5)
            ntok.append(tn)
        g = self.scr_gate.get("scr_stat") or []
        tw = P.dma("sync", CH_STAT, lambda s: s.dma_start(out=self.scr_stat[0:1, 0:T],
                                                          in_=self.STV[0:1, 0:T]), deps=ntok + g)
        NMB = self.arf(4096, 2048)
        INVB = self.arf(8192, 2048)
        tb1 = P.dma("sync", CH_R0, lambda s: s.dma_start(
            out=NMB, in_=self.scr_stat[0:1, 0:T].broadcast_to([128, T])), deps=[tw, self.xb_consumed])
        tw2 = P.dma("sync", CH_STAT, lambda s: s.dma_start(out=self.scr_stat[0:1, T:2 * T],
                                                           in_=self.STV[0:1, 2 * T:3 * T]), deps=ftok)
        tb2 = P.dma("sync", CH_R1, lambda s: s.dma_start(
            out=INVB, in_=self.scr_stat[0:1, T:2 * T].broadcast_to([128, T])), deps=[tw2, self.xb_consumed])
        self.scr_gate["scr_stat"] = [tb1, tb2]
        xnb = []
        prevv = None
        for c in range(NCH):
            eng = "gpsimd" if c % 3 == 2 else "vector"
            off = 12288 if eng == "vector" else 14336
            TM = self.arf(off, 1024)
            d1 = [xc[c], tb1, self.xb_consumed]
            if eng == "vector" and prevv is not None:
                d1.append(prevv)
            t1a = P.op(eng, lambda v, c=c, TM=TM: v.tensor_add(TM, self.XB[:, c * T:c * T + 1024], NMB[:, 0:1024]),
                       deps=d1)
            t1b = P.op(eng, lambda v, c=c, TM=TM: v.tensor_mul(self.XB[:, c * T:c * T + 1024], TM, INVB[:, 0:1024]),
                       deps=[t1a, tb2])
            t2a = P.op(eng, lambda v, c=c, TM=TM: v.tensor_add(TM, self.XB[:, c * T + 1024:(c + 1) * T],
                                                               NMB[:, 1024:2048]), deps=[t1b])
            t2b = P.op(eng, lambda v, c=c, TM=TM: v.tensor_mul(self.XB[:, c * T + 1024:(c + 1) * T], TM,
                                                               INVB[:, 1024:2048]), deps=[t2a])
            if eng == "vector":
                prevv = t2b
            xnb.append(tmax(t1b, t2b))
        self.xnb_tok = xnb
        self.xnb_all = tmax(*xnb)
        self.stat_busy = [tb1, tb2]
        return xnb

    def proj_feature(self, wsrc, wcol0, writer, n_out=NCH):
        P = self.P
        wtok = {0: self.load_wcb(wsrc, wcol0, 128, 0),
                1: self.load_wcb(wsrc, wcol0 + 128, 128, 1)}
        for do in range(n_out):
            slot = do % 2
            banks = [slot * 4 + tb for tb in range(4)]
            gdeps = self.acq(banks, [wtok[do]])
            last = None
            for k in range(NCH):
                for tb in range(4):
                    def mm(t, k=k, tb=tb, bk=banks[tb], slot=slot):
                        return t.matmul(self.bank(bk),
                                        self.WCB[slot][:, k * 128:(k + 1) * 128],
                                        self.XB[:, k * T + tb * 512: k * T + tb * 512 + 512],
                                        start=(k == 0), stop=(k == NCH - 1))
                    last = P.op("tensor", mm, deps=(gdeps + [self.xnb_tok[k]]) if tb == 0 else (),
                                inc=(k == NCH - 1 and tb == 3))
            self.wcb_busy[slot] = last
            if do + 2 < n_out:
                wtok[do + 2] = self.load_wcb(wsrc, wcol0 + (do + 2) * 128, 128, slot)
            for tb in range(4):
                cons = writer(do, tb, banks[tb], [last])
                self.rel([banks[tb]], cons)

    def proj_token(self, wsrc, wcol0, wcols, writer):
        P = self.P
        n_dob = wcols // 512
        wtok = {0: self.load_wcb(wsrc, wcol0, 512, 0)}
        if n_dob > 1:
            wtok[1] = self.load_wcb(wsrc, wcol0 + 512, 512, 1)
        for dob in range(n_dob):
            slot = dob % 2
            last = None
            for tc in range(NCH):
                bk = (tc % 2) * 4 + dob
                gdeps = self.acq([bk], [wtok[dob]])
                for k in range(NCH):
                    def mm(t, k=k, tc=tc, bk=bk, slot=slot):
                        return t.matmul(self.bank(bk),
                                        self.XB[:, k * T + tc * 128: k * T + tc * 128 + 128],
                                        self.WCB[slot][:, k * 512:(k + 1) * 512],
                                        start=(k == 0), stop=(k == NCH - 1))
                    last = P.op("tensor", mm,
                                deps=(gdeps + [self.xnb_tok[k]]) if k == 0 else [self.xnb_tok[k]],
                                inc=(k == NCH - 1))
                cons = writer(tc, dob, bk, [last])
                self.rel([bk], cons)
            self.wcb_busy[slot] = last
            if dob + 2 < n_dob:
                wtok[dob + 2] = self.load_wcb(wsrc, wcol0 + (dob + 2) * 512, 512, slot)

    def feat_writer(self, dst_fn, bias_name, scr_key):
        P = self.P

        def writer(do, tb, bk, deps):
            slot = (do * 4 + tb) % 4
            ev = self.evb(slot)
            d = list(deps)
            if self.evb_busy[slot] is not None:
                d.append(self.evb_busy[slot])
            bias = self.PBS[bias_name][:, do:do + 1]
            tok = P.op("vector", lambda v, ev=ev, bk=bk, bias=bias: v.tensor_scalar(
                ev, self.bank(bk), bias, None, ALU.add), deps=d)
            wd = [tok] + (self.scr_gate.get(scr_key) or [])

            def dfn(s, do=do, tb=tb, ev=ev):
                o, i = dst_fn(do, tb, ev)
                return s.dma_start(out=o, in_=i)
            wtok = P.dma("sync", CH_EV + slot, dfn, deps=wd)
            self.evb_busy[slot] = wtok
            self.cur_scr_writes.setdefault(scr_key, []).append(wtok)
            return [tok]
        return writer

    def tok_writer(self, dst_fn, bvb_ap, bvb_tok, scr_key):
        P = self.P

        def writer(tc, dob, bk, deps):
            slot = (tc * 4 + dob) % 4 + 4
            ev = self.evb(slot)
            d = list(deps) + [bvb_tok]
            if self.evb_busy[slot] is not None:
                d.append(self.evb_busy[slot])
            tok = P.op("vector", lambda v, ev=ev, bk=bk, dob=dob: v.tensor_add(
                ev, self.bank(bk), bvb_ap[:, dob * 512:(dob + 1) * 512]), deps=d)
            wd = [tok] + (self.scr_gate.get(scr_key) or [])

            def dfn(s, tc=tc, dob=dob, ev=ev):
                o, i = dst_fn(tc, dob, ev)
                return s.dma_start(out=o, in_=i)
            wtok = P.dma("sync", CH_EV + slot, dfn, deps=wd)
            self.evb_busy[slot] = wtok
            self.cur_scr_writes.setdefault(scr_key, []).append(wtok)
            return [tok]
        return writer

    def residual_writer(self, xsrc, xdst, bias_name):
        P = self.P
        state = {"xo": [None, None], "ev": [None, None]}

        def writer(do, tbg, bk, deps):
            xslot = (do * 4 + tbg) % 2
            XOLD = self.arf(12288 + xslot * 1024, 512)
            EVF = self.arf(14336 + xslot * 1024, 512)
            xod = [state["xo"][xslot], self.next_x_ready[do], self.xnb_all]
            txo = P.dma("sync", CH_XO0 + xslot, lambda s, do=do, tbg=tbg, XOLD=XOLD: s.dma_start(
                out=XOLD, in_=xsrc[do * 128:(do + 1) * 128, tbg * 512:(tbg + 1) * 512]), deps=xod)
            self.xsrc_reads.append(txo)
            evd = [state["ev"][xslot], txo, self.xnb_all] + list(deps)
            if bias_name is not None:
                bias = self.PBS[bias_name][:, do:do + 1]
                tev = P.op("vector", lambda v, bk=bk, XOLD=XOLD, EVF=EVF, bias=bias: v.scalar_tensor_tensor(
                    EVF, self.bank(bk), bias, XOLD, ALU.add, ALU.add), deps=evd)
            else:
                tev = P.op("vector", lambda v, bk=bk, XOLD=XOLD, EVF=EVF: v.tensor_add(
                    EVF, self.bank(bk), XOLD), deps=evd)
            wd = [tev] + (self.buf_gate.get(id(xdst)) or [])
            twr = P.dma("sync", CH_XW + do, lambda s, do=do, tbg=tbg, EVF=EVF: s.dma_start(
                out=xdst[do * 128:(do + 1) * 128, tbg * 512:(tbg + 1) * 512], in_=EVF), deps=wd)
            state["xo"][xslot] = tev
            state["ev"][xslot] = twr
            self.cur_x_writes[(do, tbg)] = twr
            return [tev]
        return writer

    def begin_stage(self):
        self.cur_scr_writes = {}
        self.cur_x_writes = {}
        self.xsrc_reads = []

    def finish_stage(self, xsrc, xdst, extra_reads, scr_keys_read, all_done_toks):
        for k in scr_keys_read:
            self.scr_gate[k] = list(tmax(*extra_reads))
        self.buf_gate[id(xsrc)] = list(tmax(*self.xsrc_reads))
        self.next_x_ready = [tmax(*(self.cur_x_writes.get((c, tb)) for tb in range(4))) for c in range(NCH)]
        self.xb_consumed = tmax(*all_done_toks)
        self.final_writes += list(self.cur_x_writes.values())
        self.buf_gate.pop(id(xdst), None)

    def emit_attn_a(self, xsrc, xdst):
        P = self.P
        self.begin_stage()
        self.emit_stats_cast(xsrc)
        self.proj_feature(self.wk, 0, self.feat_writer(
            lambda do, tb, ev: (self.a_k[do, :, tb * 512:(tb + 1) * 512], ev), "bk", "a_k"))
        self.proj_feature(self.wq, 0, self.feat_writer(
            lambda do, tb, ev: (self.a_q[do, :, tb * 512:(tb + 1) * 512], ev), "bq", "a_q"))
        BVB = self.arf(4096, 2048)
        bv_tok = P.dma("sync", CH_R2, lambda s: s.dma_start(
            out=BVB, in_=self.row_bv.broadcast_to([128, D])), deps=self.xnb_tok + [self.stat_busy[0]])

        def av_dst(tc, dob, ev):
            return (self.a_v[dob * 4:(dob + 1) * 4, :, tc, :].rearrange("h p f -> p h f"),
                    ev.rearrange("p (h f) -> p h f", f=128))
        self.proj_token(self.wv, 0, D, self.tok_writer(av_dst, BVB, bv_tok, "a_v"))
        aq_all = tmax(*self.cur_scr_writes["a_q"])
        ak_all = tmax(*self.cur_scr_writes["a_k"])
        av_all = tmax(*self.cur_scr_writes["a_v"])

        KT = lambda s: self.arb(0 + s * 2048, 2048)
        VH = lambda s: self.arb(4096 + s * 2048, 2048)
        QT = lambda s: self.arb(8192 + s * 1024, 1024)
        RB = self.arf(10240, 1024)
        expST = lambda kt: self.XB[:, kt * 1024:(kt + 1) * 1024]
        PREF = lambda h: self.XB[:, 16384 + h * 1024: 16384 + (h + 1) * 1024]
        phase_gate = tmax(aq_all, ak_all, av_all, self.xnb_all, *self.stat_busy, bv_tok)

        kt_busy = [None, None]
        vh_busy = [None, None]
        qt_busy = [None, None]
        exp_free = {}
        pref_gate = {}
        a3_reads = []
        res_writer = self.residual_writer(xsrc, xdst, "bo")
        for qbp in range(2):
            for h in range(H):
                slot = h % 2
                tk = P.dma("sync", CH_R0 + slot, lambda s, h=h, slot=slot: s.dma_start(
                    out=KT(slot)[:, :], in_=self.a_k[h, :, :]),
                    deps=[kt_busy[slot], phase_gate])
                tv = P.dma("sync", CH_R2 + slot, lambda s, h=h, slot=slot: s.dma_start(
                    out=VH(slot)[:, :].rearrange("p (kt f) -> p kt f", f=128),
                    in_=self.a_v[h]), deps=[vh_busy[slot], phase_gate])
                tq = P.dma("sync", CH_R4 + slot, lambda s, h=h, qbp=qbp, slot=slot: s.dma_start(
                    out=QT(slot)[:, :], in_=self.a_q[h, :, qbp * 1024:(qbp + 1) * 1024]),
                    deps=[qt_busy[slot], phase_gate])
                a3_reads += [tk, tv, tq]
                exp_toks = {}
                last_st = None
                for kt in range(NCH):
                    pdi = kt % 2
                    deps = self.acq([pdi * 2, pdi * 2 + 1], [tk, tq])
                    P.op("tensor", lambda t, kt=kt, pdi=pdi, slot=slot: t.matmul(
                        self.PD[pdi][:, 0:512], KT(slot)[:, kt * 128:(kt + 1) * 128],
                        QT(slot)[:, 0:512], start=True, stop=True), deps=deps, inc=False)
                    m1 = P.op("tensor", lambda t, kt=kt, pdi=pdi, slot=slot: t.matmul(
                        self.PD[pdi][:, 512:1024], KT(slot)[:, kt * 128:(kt + 1) * 128],
                        QT(slot)[:, 512:1024], start=True, stop=True))
                    te = P.op("scalar", lambda s, kt=kt, pdi=pdi: s.activation(
                        expST(kt), self.PD[pdi][:, :], AF.Exp, scale=1.0 / 128.0),
                        deps=[m1, exp_free.get(kt), phase_gate if (qbp == 0 and h == 0) else None])
                    exp_toks[kt] = te
                    last_st = m1
                    self.rel([pdi * 2, pdi * 2 + 1], [te])
                av0 = self.acq([4], [tv])
                av1 = self.acq([5])
                sm = self.acq([6, 7])
                last_av = None
                for kt in range(NCH):
                    P.op("tensor", lambda t, kt=kt, slot=slot: t.matmul(
                        self.PD[2][:, 0:512], VH(slot)[:, kt * 128:(kt + 1) * 128],
                        expST(kt)[:, 0:512], start=(kt == 0), stop=(kt == NCH - 1)),
                        deps=(av0 if kt == 0 else []) + [exp_toks[kt]], inc=False)
                    P.op("tensor", lambda t, kt=kt, slot=slot: t.matmul(
                        self.PD[2][:, 512:1024], VH(slot)[:, kt * 128:(kt + 1) * 128],
                        expST(kt)[:, 512:1024], start=(kt == 0), stop=(kt == NCH - 1)),
                        deps=av1 if kt == 0 else (), inc=False)
                    P.op("tensor", lambda t, kt=kt: t.matmul(
                        self.PD[3][:, 0:512], self.ALLONES[:, :], expST(kt)[:, 0:512],
                        start=(kt == 0), stop=(kt == NCH - 1)), deps=sm if kt == 0 else (), inc=False)
                    s1 = P.op("tensor", lambda t, kt=kt: t.matmul(
                        self.PD[3][:, 512:1024], self.ALLONES[:, :], expST(kt)[:, 512:1024],
                        start=(kt == 0), stop=(kt == NCH - 1)))
                    exp_free[kt] = s1
                    last_av = s1
                r0 = P.op("vector", lambda v: v.reciprocal(RB[:, 0:512], self.PD[3][:, 0:512]),
                          deps=[last_av, phase_gate if (qbp == 0 and h == 0) else None])
                r1 = P.op("vector", lambda v: v.reciprocal(RB[:, 512:1024], self.PD[3][:, 512:1024]),
                          deps=[r0])
                self.rel([6, 7], [r1])
                e0 = P.op("vector", lambda v, h=h: v.tensor_mul(
                    PREF(h)[:, 0:512], self.PD[2][:, 0:512], RB[:, 0:512]),
                    deps=[last_av, r0] + (pref_gate.get(h) or []))
                e1 = P.op("vector", lambda v, h=h: v.tensor_mul(
                    PREF(h)[:, 512:1024], self.PD[2][:, 512:1024], RB[:, 512:1024]), deps=[r1])
                self.rel([4], [e0])
                self.rel([5], [e1])
                kt_busy[slot] = last_st
                qt_busy[slot] = last_st
                vh_busy[slot] = last_av
                pref_gate[h] = [e0, e1]
            wtok = {0: self.load_wcb(self.wo, 0, 128, 0), 1: self.load_wcb(self.wo, 128, 128, 1)}
            new_pref_gate = {}
            for do in range(NCH):
                slot = do % 2
                banks = [slot * 4 + qb for qb in range(2)]
                gdeps = self.acq(banks, [wtok[do]])
                for h in range(H):
                    gdeps += pref_gate.get(h) or []
                last = None
                for k in range(NCH):
                    for qb in range(2):
                        def mm(t, k=k, qb=qb, bk=banks[qb], slot=slot):
                            return t.matmul(self.bank(bk),
                                            self.WCB[slot][:, k * 128:(k + 1) * 128],
                                            PREF(k)[:, qb * 512:(qb + 1) * 512],
                                            start=(k == 0), stop=(k == NCH - 1))
                        last = P.op("tensor", mm, deps=gdeps if (k == 0 and qb == 0) else (),
                                    inc=(k == NCH - 1 and qb == 1))
                self.wcb_busy[slot] = last
                for h in range(H):
                    new_pref_gate.setdefault(h, []).append(last)
                if do + 2 < NCH:
                    wtok[do + 2] = self.load_wcb(self.wo, (do + 2) * 128, 128, slot)
                for qb in range(2):
                    cons = res_writer(do, qbp * 2 + qb, banks[qb], [last])
                    self.rel([banks[qb]], cons)
            pref_gate = new_pref_gate
        all_done = [tmax(*(pref_gate[h] for h in range(H)), *self.cur_x_writes.values())]
        self.finish_stage(xsrc, xdst, a3_reads, ["a_q", "a_k", "a_v"], all_done)

    def emit_ffn(self, xsrc, xdst):
        P = self.P
        self.begin_stage()
        self.emit_stats_cast(xsrc)
        lg_evac = []
        for tb in range(4):
            bk = 4 + tb % 2
            gdeps = self.acq([bk])
            last = None
            for k in range(NCH):
                def mm(t, k=k, tb=tb, bk=bk):
                    return t.matmul(self.PD[bk // 2][0:8, (bk % 2) * 512:(bk % 2) * 512 + 512],
                                    self.RWT[:, k * 8:(k + 1) * 8],
                                    self.XB[:, k * T + tb * 512: k * T + tb * 512 + 512],
                                    start=(k == 0), stop=(k == NCH - 1))
                last = P.op("tensor", mm, deps=gdeps + [self.xnb_tok[k]], inc=(k == NCH - 1))
            ev = P.op("vector", lambda v, tb=tb, bk=bk: v.tensor_copy(
                self.arf(0, 2048)[0:8, tb * 512:(tb + 1) * 512],
                self.PD[bk // 2][0:8, (bk % 2) * 512:(bk % 2) * 512 + 512]),
                deps=[last] + self.sq_mm_done)
            self.rel([bk], [ev])
            lg_evac.append(ev)
        LGTF = self.arf(0, 2048)
        self.tap("lgt", LGTF[0:8, :], lg_evac)
        w8 = self.W8S
        w8_done = []
        for ti in range(NCH):
            tb = ti // 4
            d0 = self.acq([6], [lg_evac[tb]])
            t_tr = P.op("tensor", lambda t, ti=ti: t.transpose(
                self.PD[3][:, 0:8], LGTF[0:8, ti * 128:(ti + 1) * 128], self.EYEF[0:8, 0:8]), deps=d0)
            L = self.PD[3][:, 0:8]
            m1, nm1, m2, rc = w8[:, 0:1], w8[:, 1:2], w8[:, 2:3], w8[:, 3:4]
            E, Msk, Ew, W8 = w8[:, 8:16], w8[:, 16:24], w8[:, 24:32], w8[:, 32:40]
            v1 = P.op("vector", lambda v: v.reduce_max(m1, L, axis=AX.X),
                      deps=[t_tr, self.xnb_all] if ti == 0 else [t_tr])
            v2 = P.op("vector", lambda v: v.tensor_scalar_mul(nm1, m1, -1.0), deps=[v1], selfwait=True)
            v3 = P.op("vector", lambda v: v.tensor_scalar(Msk, L, m1, -1e30, ALU.is_ge, ALU.mult),
                      deps=[v1], selfwait=True)
            v4 = P.op("vector", lambda v: v.tensor_tensor(Ew, L, Msk, ALU.add), deps=[v3], selfwait=True)
            v5 = P.op("vector", lambda v: v.reduce_max(m2, Ew, axis=AX.X), deps=[v4], selfwait=True)
            s1 = P.op("scalar", lambda s: s.activation(E, L, AF.Exp, bias=nm1), deps=[v2, v5])
            v6 = P.op("vector", lambda v: v.tensor_scalar(Msk, L, m2, 1.0, ALU.is_ge, ALU.mult),
                      deps=[s1, v5], selfwait=True)
            v7 = P.op("vector", lambda v: v.tensor_mul(Ew, E, Msk), deps=[v6], selfwait=True)
            v8 = P.op("vector", lambda v: v.tensor_reduce(rc, Ew, axis=AX.X, op=ALU.add),
                      deps=[v7], selfwait=True)
            v9 = P.op("vector", lambda v: v.reciprocal(rc, rc), deps=[v8], selfwait=True)
            va = P.op("vector", lambda v: v.tensor_scalar(W8, Ew, rc, None, ALU.mult),
                      deps=[v9], selfwait=True)
            self.rel([6], [s1, va])
            d1 = self.acq([7])
            t_tw = P.op("tensor", lambda t: t.transpose(self.PD[3][0:8, 512:640], W8, self.EYEF[:, :]),
                        deps=d1 + [va])
            vb = P.op("vector", lambda v, ti=ti: v.tensor_copy(
                self.WTB[0:8, ti * 128:(ti + 1) * 128], self.PD[3][0:8, 512:640]), deps=[t_tw])
            self.rel([7], [vb])
            w8_done.append(vb)
            if ti == 0 and self.taps_on:
                ttok = self.tap("w8all", w8[:, 0:40], [va, s1])
                self.bank_state[6] = self.bank_state[6] + [ttok]
        WB = self.arf(4096, 2048)
        gw = self.scr_gate.get("scr_w8") or []
        t_w = P.dma("sync", CH_STAT, lambda s: s.dma_start(out=self.scr_w8[:, :], in_=self.WTB[:, :]),
                    deps=w8_done + gw)
        wb_toks = []
        for e in range(8):
            wb_toks.append(P.dma("sync", CH_R3, lambda s, e=e: s.dma_start(
                out=WB[e * 16:(e + 1) * 16, :],
                in_=self.scr_w8[e:e + 1, :].broadcast_to([16, T])),
                deps=[t_w, self.xnb_all, self.stat_busy[0]]))
        wb_all = tmax(*wb_toks)
        self.scr_gate["scr_w8"] = wb_toks
        self.tap("wtb", self.WTB[:, :], w8_done)
        self.tap("wb", WB[:, :], wb_toks)
        AXW = self.arb(8192, 2048)
        axw_toks = []
        for tb in range(4):
            bk = 4 + tb % 2
            gdeps = self.acq([bk])
            last = None
            for k in range(NCH):
                def mm(t, k=k, tb=tb, bk=bk):
                    return t.matmul(self.bank(bk), self.ATW[:, k * 128:(k + 1) * 128],
                                    self.XB[:, k * T + tb * 512: k * T + tb * 512 + 512],
                                    start=(k == 0), stop=(k == NCH - 1))
                last = P.op("tensor", mm, deps=gdeps + [self.xnb_tok[k]], inc=(k == NCH - 1))
            tx = P.op("vector", lambda v, tb=tb, bk=bk: v.scalar_tensor_tensor(
                AXW[:, tb * 512:(tb + 1) * 512], self.bank(bk), 2.0,
                WB[:, tb * 512:(tb + 1) * 512], ALU.mult, ALU.mult),
                deps=[last, wb_all, self.xnb_all, self.stat_busy[1]])
            self.rel([bk], [tx])
            axw_toks.append(tx)
        self.tap("axw", AXW[:, :], axw_toks)
        res_writer = self.residual_writer(xsrc, xdst, None)
        wtok = {0: self.load_wcb(self.wft, 0, 128, 0), 1: self.load_wcb(self.wft, 128, 128, 1)}
        for do in range(NCH):
            slot = do % 2
            banks = [slot * 4 + tb for tb in range(4)]
            gdeps = self.acq(banks, [wtok[do]])
            for k in range(NCH):
                for tb in range(4):
                    def mm(t, k=k, tb=tb, bk=banks[tb], slot=slot):
                        return t.matmul(self.bank(bk),
                                        self.WCB[slot][:, k * 128:(k + 1) * 128],
                                        self.XB[:, k * T + tb * 512: k * T + tb * 512 + 512],
                                        start=(k == 0), stop=False)
                    P.op("tensor", mm, deps=(gdeps + [self.xnb_tok[k]]) if tb == 0 else (), inc=False)
            last = None
            for tb in range(4):
                def mmbt(t, tb=tb, bk=banks[tb], do=do):
                    return t.matmul(self.bank(bk), self.BTW[:, do * 128:(do + 1) * 128],
                                    AXW[:, tb * 512:(tb + 1) * 512], start=False, stop=True)
                last = P.op("tensor", mmbt, deps=[axw_toks[tb]], inc=(tb == 3))
            self.wcb_busy[slot] = last
            if do + 2 < NCH:
                wtok[do + 2] = self.load_wcb(self.wft, (do + 2) * 128, 128, slot)
            for tb in range(4):
                cons = res_writer(do, tb, banks[tb], [last])
                self.rel([banks[tb]], cons)
        all_done = [tmax(*self.cur_x_writes.values())]
        self.finish_stage(xsrc, xdst, [wb_all], [], all_done)

    def emit_attn_b(self, xsrc, xdst):
        P = self.P
        self.begin_stage()
        self.emit_stats_cast(xsrc)
        self.proj_feature(self.wqkv, 0, self.feat_writer(
            lambda do, tb, ev: (self.b_q[tb * 4:(tb + 1) * 4, :, do, :].rearrange("c p f -> p c f"),
                                ev.rearrange("p (c f) -> p c f", f=128)), "bqkv_q", "b_q"))
        self.proj_feature(self.wqkv, D, self.feat_writer(
            lambda do, tb, ev: (self.b_k[tb * 4:(tb + 1) * 4, :, do, :].rearrange("c p f -> p c f"),
                                ev.rearrange("p (c f) -> p c f", f=128)), "bqkv_k", "b_k"))
        BVB = self.arf(4096, 2048)
        bv_tok = P.dma("sync", CH_R2, lambda s: s.dma_start(
            out=BVB, in_=self.row_bqv.broadcast_to([128, D])), deps=self.xnb_tok + [self.stat_busy[0]])
        self.proj_token(self.wqkv, 2 * D, D, self.tok_writer(
            lambda tc, dob, ev: (self.b_v[tc, :, dob * 512:(dob + 1) * 512], ev), BVB, bv_tok, "b_v"))
        bq_all = tmax(*self.cur_scr_writes["b_q"])
        bk_all = tmax(*self.cur_scr_writes["b_k"])
        bv_all = tmax(*self.cur_scr_writes["b_v"])
        phase_gate = tmax(bq_all, bk_all, bv_all, self.xnb_all, *self.stat_busy, bv_tok)

        PREFB = lambda sl: self.XB[:, sl * 8192:(sl + 1) * 8192]
        S_sb = self.XBF[:, 8192:10240]
        SSB = self.XB[:, 20480:22528]
        QDB = self.XB[:, 22528:24576]
        STB = self.XB[:, 24576:26624]
        KDB = self.XB[:, 26624:28672]
        VC = lambda sl: self.XB[:, 28672 + sl * 2048: 28672 + (sl + 1) * 2048]
        QC = lambda sl: self.arb(0 + sl * 2048, 2048)
        KC = lambda sl: self.arb(4096 + sl * 2048, 2048)

        z0 = P.op("vector", lambda v: v.memset(S_sb[:, :], 0.0), deps=[phase_gate])
        z1 = P.op("vector", lambda v: v.memset(SSB[:, :], 0.0), deps=[phase_gate])
        state_tok = {h: [z0, z1] for h in range(H)}
        qc_busy = [None, None]
        kc_busy = [None, None]
        vc_busy = [None, None]
        stb_gate = {}
        qdb_gate = {}
        kdb_gate = {}
        prefb_gate = [None, None]
        b3_reads = []
        res_writer = self.residual_writer(xsrc, xdst, "bout")
        o_evac = {}
        for c in range(NCH):
            slot = c % 2
            tqc = P.dma("sync", CH_R0 + slot, lambda s, c=c, slot=slot: s.dma_start(
                out=QC(slot)[:, :].rearrange("p (h f) -> p h f", f=128),
                in_=self.b_q[c]), deps=[qc_busy[slot], phase_gate])
            tkc = P.dma("sync", CH_R2 + slot, lambda s, c=c, slot=slot: s.dma_start(
                out=KC(slot)[:, :].rearrange("p (h f) -> p h f", f=128),
                in_=self.b_k[c]), deps=[kc_busy[slot], phase_gate])
            tvc = P.dma("sync", CH_R4 + slot, lambda s, c=c, slot=slot: s.dma_start(
                out=VC(slot)[:, :], in_=self.b_v[c]), deps=[vc_busy[slot], phase_gate])
            b3_reads += [tqc, tkc, tvc]
            chunk_vc_uses = []
            for h in range(H):
                hp = h % 2
                st_b, o_b, u_b = hp, 2 + hp, 4 + hp
                hs = slice(h * 128, (h + 1) * 128)
                m_st = P.op("tensor", lambda t, bk=st_b, hs=hs, slot=slot: t.matmul(
                    self.bank(bk)[:, 0:128], KC(slot)[:, hs], QC(slot)[:, hs],
                    start=True, stop=True), deps=self.acq([st_b], [tqc, tkc]))
                v_dec = P.op("vector", lambda v, bk=st_b, hs=hs: v.tensor_mul(
                    STB[:, hs], self.bank(bk)[:, 0:128], self.INTRA[:, hs]),
                    deps=[m_st, phase_gate if c == 0 else None] + (stb_gate.get(h) or []))
                self.rel([st_b], [v_dec])
                kd_bank = 6 + hp
                kd_ap = self.PDB[:, (kd_bank - 6) * 1024:(kd_bank - 6) * 1024 + 128]
                m_tr = P.op("tensor", lambda t, kd_ap=kd_ap, hs=hs, slot=slot: t.transpose(
                    kd_ap, KC(slot)[:, hs], self.EYEB[:, :]), deps=self.acq([kd_bank], [tkc]))
                s_kd = P.op("scalar", lambda s, kd_ap=kd_ap, hs=hs, h=h: s.mul(
                    KDB[:, hs], kd_ap, self.SLP[:, 48 + h:49 + h]),
                    deps=[m_tr, phase_gate if c == 0 else None] + (kdb_gate.get(h) or []))
                self.rel([kd_bank], [s_kd])
                v_qd = P.op("vector", lambda v, hs=hs, slot=slot: v.tensor_mul(
                    QDB[:, hs], QC(slot)[:, hs], self.QDECF[:, hs]),
                    deps=[tqc, phase_gate if c == 0 else None] + (qdb_gate.get(h) or []))
                m_oi = P.op("tensor", lambda t, bk=o_b, hs=hs, slot=slot: t.matmul(
                    self.bank(bk)[:, 0:128], VC(slot)[:, hs], STB[:, hs],
                    start=True, stop=False), deps=self.acq([o_b], [v_dec, tvc]))
                m_oe = P.op("tensor", lambda t, bk=o_b, hs=hs: t.matmul(
                    self.bank(bk)[:, 0:128], SSB[:, hs], QDB[:, hs],
                    start=False, stop=True), deps=[v_qd] + state_tok[h])
                sl_p = (c // 4) % 2
                s_oe = P.op("scalar", lambda s, bk=o_b, h=h, c=c, sl_p=sl_p: s.copy(
                    PREFB(sl_p)[:, h * 512 + (c % 4) * 128: h * 512 + (c % 4) * 128 + 128],
                    self.bank(bk)[:, 0:128]),
                    deps=[m_oe, prefb_gate[sl_p]])
                self.rel([o_b], [s_oe])
                o_evac.setdefault(c // 4, []).append(s_oe)
                m_up = P.op("tensor", lambda t, bk=u_b, hs=hs, slot=slot: t.matmul(
                    self.bank(bk)[:, 0:128], KDB[:, hs], VC(slot)[:, hs],
                    start=True, stop=True), deps=self.acq([u_b], [s_kd, tvc]))
                v_st = P.op("vector", lambda v, bk=u_b, hs=hs, h=h: v.scalar_tensor_tensor(
                    S_sb[:, hs], S_sb[:, hs], self.SLP[:, 32 + h:33 + h],
                    self.bank(bk)[:, 0:128], ALU.mult, ALU.add), deps=[m_up] + state_tok[h])
                s_cast = P.op("scalar", lambda s, hs=hs: s.copy(SSB[:, hs], S_sb[:, hs]),
                              deps=[v_st, m_oe])
                self.rel([u_b], [v_st])
                state_tok[h] = [s_cast]
                stb_gate[h] = [m_oi]
                qdb_gate[h] = [m_oe]
                kdb_gate[h] = [m_up]
                chunk_vc_uses += [m_oi, m_up]
            qc_busy[slot] = tmax(*(qdb_gate[h] for h in range(H)))
            kc_busy[slot] = tmax(*(kdb_gate[h] for h in range(H)))
            vc_busy[slot] = tmax(*chunk_vc_uses)
            if c % 4 == 3:
                tbg = c // 4
                sl_p = tbg % 2
                wtok = {0: self.load_wcb(self.wout, 0, 128, 0), 1: self.load_wcb(self.wout, 128, 128, 1)}
                pf_done = o_evac[tbg]
                gates = []
                for do in range(NCH):
                    slot_w = do % 2
                    bk = do % 4
                    gdeps = self.acq([bk], [wtok[do]]) + pf_done
                    last = None
                    for k in range(NCH):
                        def mm(t, k=k, bk=bk, slot_w=slot_w, sl_p=sl_p):
                            return t.matmul(self.bank(bk),
                                            self.WCB[slot_w][:, k * 128:(k + 1) * 128],
                                            PREFB(sl_p)[:, k * 512:(k + 1) * 512],
                                            start=(k == 0), stop=(k == NCH - 1))
                        last = P.op("tensor", mm, deps=gdeps if k == 0 else (), inc=(k == NCH - 1))
                    self.wcb_busy[slot_w] = last
                    gates.append(last)
                    if do + 2 < NCH:
                        wtok[do + 2] = self.load_wcb(self.wout, (do + 2) * 128, 128, slot_w)
                    cons = res_writer(do, tbg, bk, [last])
                    self.rel([bk], cons)
                prefb_gate[sl_p] = tmax(*gates)
        all_done = [tmax(*self.cur_x_writes.values())]
        self.finish_stage(xsrc, xdst, b3_reads, ["b_q", "b_k", "b_v"], all_done)


def build_program(n_stages=4):
    kb = KB(n_stages)
    with kb.es:
        kb.emit_all()
        block = kb.es.enter_context(kb.nc.Block())
        kb.P.build(block)
    return kb.nc



def _host_consts():
    import ml_dtypes
    i = np.arange(128, dtype=np.float32)
    c = {}
    c["c_eye_b"] = np.eye(128, dtype=np.float32)
    c["c_eye_f"] = np.eye(128, dtype=np.float32)
    dif = i[:, None] - i[None, :]
    c["c_dif"] = np.where(dif <= 0, dif, -1.0).astype(np.float32)
    c["c_mask"] = (i[:, None] <= i[None, :]).astype(np.float32)
    c["c_rowidx"] = np.ascontiguousarray(np.broadcast_to(i[None, :], (128, 128)))
    c["c_col128m"] = np.ascontiguousarray((128.0 - i)[:, None])
    c["c_onescol"] = np.ones((128, 1), dtype=np.float32)
    c["c_allones"] = np.ones((128, 128), dtype=np.float32)
    for k in ("c_eye_b", "c_onescol", "c_allones"):
        c[k] = c[k].astype(ml_dtypes.bfloat16)
    return c


def _host_prep(inputs):
    x = np.asarray(inputs["L0_attn_a_x"], dtype=np.float32)
    common = {
        "wq": np.asarray(inputs["L0_attn_a_Wq"], dtype=np.float32),
        "wk": np.asarray(inputs["L0_attn_a_Wk"], dtype=np.float32),
        "wv": np.asarray(inputs["L0_attn_a_Wv"], dtype=np.float32),
        "wo": np.asarray(inputs["L0_attn_a_Wo"], dtype=np.float32),
        "wqkv": np.asarray(inputs["L0_attn_b_Wqkv"], dtype=np.float32),
        "wout": np.asarray(inputs["L0_attn_b_Wout"], dtype=np.float32),
        "wft": np.ascontiguousarray(np.asarray(inputs["L0_ffn_weight"], dtype=np.float32).T),
        "slopes": np.asarray(inputs["L0_attn_b_slopes"], dtype=np.float32).reshape(1, H),
        "row_bv": np.asarray(inputs["L0_attn_a_bv"], dtype=np.float32).reshape(1, D),
        "row_bqv": np.ascontiguousarray(
            np.asarray(inputs["L0_attn_b_bqkv"], dtype=np.float32)[2 * D:].reshape(1, D)),
    }
    rwt = np.asarray(inputs["L0_ffn_router_w"], dtype=np.float32).T
    common["rwt"] = np.ascontiguousarray(rwt.reshape(NCH, 128, 8).transpose(1, 0, 2).reshape(128, 128))
    at = np.asarray(inputs["L0_ffn_lora_A"], dtype=np.float32)
    atf = at.transpose(2, 0, 1).reshape(D, 128)
    common["atw"] = np.ascontiguousarray(atf.reshape(NCH, 128, 128).transpose(1, 0, 2).reshape(128, NCH * 128))
    bt = np.asarray(inputs["L0_ffn_lora_B"], dtype=np.float32)
    common["btw"] = np.ascontiguousarray(bt.transpose(0, 2, 1).reshape(128, D))
    for name, arr in (("bq", inputs["L0_attn_a_bq"]), ("bk", inputs["L0_attn_a_bk"]),
                      ("bo", inputs["L0_attn_a_bo"]), ("bout", inputs["L0_attn_b_bout"])):
        common[f"pb_{name}"] = np.ascontiguousarray(
            np.asarray(arr, dtype=np.float32).reshape(NCH, 128).T)
    bqkv = np.asarray(inputs["L0_attn_b_bqkv"], dtype=np.float32)
    common["pb_bqkv_q"] = np.ascontiguousarray(bqkv[0:D].reshape(NCH, 128).T)
    common["pb_bqkv_k"] = np.ascontiguousarray(bqkv[D:2 * D].reshape(NCH, 128).T)
    common.update(_host_consts())
    in_maps = []
    for core in range(NCORE):
        xm = np.ascontiguousarray(x[core * NB:(core + 1) * NB].transpose(0, 2, 1))
        m = dict(common)
        m["xin"] = xm
        in_maps.append(m)
    return in_maps


_BUILD_CACHE = {}


def kernel(**inputs):
    n_stages = 4
    if n_stages not in _BUILD_CACHE:
        _BUILD_CACHE[n_stages] = build_program(n_stages)
    nc = _BUILD_CACHE[n_stages]
    in_maps = _host_prep(inputs)
    res = run_bass_kernel_spmd(nc, in_maps, core_ids=list(range(NCORE)))
    outs = []
    for core in range(NCORE):
        y = np.asarray(res.results[core]["y"])
        outs.append(y.transpose(0, 2, 1))
    return np.ascontiguousarray(np.concatenate(outs, axis=0))


# version 03
# speedup vs baseline: 1.1388x; 1.1388x over previous
import contextlib
import numpy as np
import concourse.bass as bass
import concourse.mybir as mybir
from concourse import bass_utils
from concourse.bass_utils import run_bass_kernel_spmd

import os as _os
if _os.environ.get("K_LDWOPT", "1") == "1" and not getattr(bass_utils, "_ldw_opt_patched", False):
    bass_utils._ldw_opt_patched = True
    _orig_bvo = bass_utils.bir_verify_and_optimise

    def _bvo_ldwopt(tmpdir, inp="bir.json", outp="file.neff", arch=None, *, dve_root=None):
        real_run = bass_utils.run_command

        def run2(cmd, **kw):
            cmd = [c.replace("--enable-ldw-opt=false", "--enable-ldw-opt=true")
                   if isinstance(c, str) else c for c in cmd]
            return real_run(cmd, **kw)
        bass_utils.run_command = run2
        try:
            return _orig_bvo(tmpdir, inp, outp, arch, dve_root=dve_root)
        finally:
            bass_utils.run_command = real_run
    bass_utils.bir_verify_and_optimise = _bvo_ldwopt

F32 = mybir.dt.float32
BF16 = mybir.dt.bfloat16
AF = mybir.ActivationFunctionType
ALU = mybir.AluOpType
AX = mybir.AxisListType

D = 2048
T = 2048
NCH = 16
H = 16
NB = 2
NCORE = 8
EPS = 1e-5

ENGINES = ("tensor", "vector", "scalar", "gpsimd", "sync")

CH_XC = 0
CH_W0 = 16
CH_W2 = 52
CH_SMALL = 18
CH_EV = 19
CH_XW = 27
CH_R0, CH_R1, CH_R2, CH_R3, CH_R4, CH_R5 = 43, 44, 45, 46, 47, 48
CH_XO0, CH_XO1 = 49, 50
CH_STAT = 51
N_DMA_CH = 54


def tmax(*toks):
    best = {}
    for t in toks:
        if t is None:
            continue
        if isinstance(t, list):
            for k, v in t:
                if best.get(k, -1) < v:
                    best[k] = v
            continue
        k, v = t
        if best.get(k, -1) < v:
            best[k] = v
    return [(k, v) for k, v in best.items()]


class Prog:
    def __init__(self, nc, es, n_dma_chans):
        self.nc = nc
        self.streams = {e: [] for e in ENGINES}
        self.counts = {e: 0 for e in ENGINES}
        self.sems = {e: es.enter_context(nc.semaphore(f"sem_{e}")) for e in ENGINES}
        self.dma_sems = [es.enter_context(nc.semaphore(f"dsem{i}")) for i in range(n_dma_chans)]
        self.dma_counts = [0] * n_dma_chans
        self.warnings = []

    def _check(self, deps):
        for key, val in deps:
            if isinstance(key, tuple) and key[0] == "dma":
                if val != self.dma_counts[key[1]]:
                    self.warnings.append(f"stale dma wait chan {key[1]}: {val} != {self.dma_counts[key[1]]}")

    def op(self, eng, fn, deps=(), inc=True, selfwait=False):
        deps = tmax(*deps) if deps else ()
        if deps:
            self._check(deps)
        tok = None
        if inc:
            self.counts[eng] += 1
            tok = (eng, self.counts[eng])
        self.streams[eng].append((fn, deps, eng if inc else None, selfwait))
        return tok

    def dma(self, queue, chan, fn, deps=()):
        deps = tmax(*deps) if deps else ()
        if deps:
            self._check(deps)
        self.dma_counts[chan] += 16
        tok = (("dma", chan), self.dma_counts[chan])
        self.streams[queue].append((fn, deps, ("dma", chan), False))
        return tok

    def _sem_of(self, key):
        if isinstance(key, tuple) and key[0] == "dma":
            return self.dma_sems[key[1]]
        return self.sems[key]

    def build(self, block):
        for eng_name in ENGINES:
            stream = self.streams[eng_name]
            deco = getattr(block, eng_name)

            def make_body(eng_name, stream):
                def body(e):
                    satisfied = {}
                    for fn, deps, inc_key, selfwait in stream:
                        for key, val in deps:
                            if key == eng_name:
                                if not selfwait:
                                    continue
                            if satisfied.get(key, -1) >= val:
                                continue
                            e.wait_ge(self._sem_of(key), val)
                            satisfied[key] = val
                        instr = fn(e)
                        if inc_key is not None and instr is not None:
                            if isinstance(inc_key, tuple):
                                instr.then_inc(self._sem_of(inc_key), 16)
                            else:
                                instr.then_inc(self.sems[inc_key], 1)
                return body

            deco(make_body(eng_name, stream))


class KB:
    def __init__(self, n_stages=4):
        self.n_stages = n_stages
        nc = bass.Bass("TRN2", target_bir_lowering=False, debug=False)
        self.nc = nc
        self.es = contextlib.ExitStack()
        dt = nc.dram_tensor

        self.xin = dt("xin", [NB, D, T], F32, kind="ExternalInput").ap()
        self.wq = dt("wq", [D, D], F32, kind="ExternalInput").ap()
        self.wk = dt("wk", [D, D], F32, kind="ExternalInput").ap()
        self.wv = dt("wv", [D, D], F32, kind="ExternalInput").ap()
        self.wo = dt("wo", [D, D], F32, kind="ExternalInput").ap()
        self.wqkv = dt("wqkv", [D, 3 * D], F32, kind="ExternalInput").ap()
        self.wout = dt("wout", [D, D], F32, kind="ExternalInput").ap()
        self.wft = dt("wft", [D, D], F32, kind="ExternalInput").ap()
        self.rwt = dt("rwt", [128, 128], F32, kind="ExternalInput").ap()
        self.atw = dt("atw", [128, NCH * 128], F32, kind="ExternalInput").ap()
        self.btw = dt("btw", [128, D], F32, kind="ExternalInput").ap()
        self.slopes = dt("slopes", [1, H], F32, kind="ExternalInput").ap()
        self.pb = {}
        for name in ("bq", "bk", "bo", "bqkv_q", "bqkv_k", "bout"):
            self.pb[name] = dt(f"pb_{name}", [128, NCH], F32, kind="ExternalInput").ap()
        self.row_bv = dt("row_bv", [1, D], F32, kind="ExternalInput").ap()
        self.row_bqv = dt("row_bqv", [1, D], F32, kind="ExternalInput").ap()
        self.c_eye_b = dt("c_eye_b", [128, 128], BF16, kind="ExternalInput").ap()
        self.c_eye_f = dt("c_eye_f", [128, 128], F32, kind="ExternalInput").ap()
        self.c_dif = dt("c_dif", [128, 128], F32, kind="ExternalInput").ap()
        self.c_mask = dt("c_mask", [128, 128], F32, kind="ExternalInput").ap()
        self.c_rowidx = dt("c_rowidx", [128, 128], F32, kind="ExternalInput").ap()
        self.c_col128m = dt("c_col128m", [128, 1], F32, kind="ExternalInput").ap()
        self.c_onescol = dt("c_onescol", [128, 1], BF16, kind="ExternalInput").ap()
        self.c_allones = dt("c_allones", [128, 128], BF16, kind="ExternalInput").ap()

        self.y = dt("y", [NB, D, T], F32, kind="ExternalOutput").ap()
        self.xs = [dt(f"xs{i}", [D, T], F32, kind="Internal").ap() for i in range(2)]
        self.a_q = dt("a_q", [H, 128, T], BF16, kind="Internal").ap()
        self.a_k = dt("a_k", [H, 128, T], BF16, kind="Internal").ap()
        self.a_v = dt("a_v", [H, 128, NCH, 128], BF16, kind="Internal").ap()
        self.b_q = dt("b_q", [NCH, 128, H, 128], BF16, kind="Internal").ap()
        self.b_k = dt("b_k", [NCH, 128, H, 128], BF16, kind="Internal").ap()
        self.b_v = dt("b_v", [NCH, 128, D], BF16, kind="Internal").ap()
        self.scr_stat = dt("scr_stat", [1, 2 * T], F32, kind="Internal").ap()
        self.scr_w8 = dt("scr_w8", [8, T], F32, kind="Internal").ap()

        es = self.es
        sb = lambda name, shape, dtyp: es.enter_context(nc.sbuf_tensor(name, shape, dtyp))
        self.XB = sb("XB", [128, NCH * T], BF16)
        self.XBF = self.XB.bitcast(F32)
        self.AR = sb("AR", [128, 18432], BF16)
        self.ARF = self.AR.bitcast(F32)
        self.WCB = [sb(f"WCB{i}", [128, NCH * 512], BF16) for i in range(2)]
        self.STV = sb("STV", [1, 3 * T], F32)
        self.EYEB = sb("EYEB", [128, 128], BF16)
        self.EYEF = sb("EYEF", [128, 128], F32)
        self.DIF = sb("DIF", [128, 128], F32)
        self.MASK = sb("MASK", [128, 128], F32)
        self.ROWIDX = sb("ROWIDX", [128, 128], F32)
        self.COLM = sb("COLM", [128, 1], F32)
        self.ONESC = sb("ONESC", [128, 1], BF16)
        self.ALLONES = sb("ALLONES", [128, 128], BF16)
        self.SLP = sb("SLP", [128, 64], F32)
        self.INTRA = sb("INTRA", [128, H * 128], BF16)
        self.QDECF = sb("QDECF", [128, H * 128], BF16)
        self.RWT = sb("RWT", [128, 128], BF16)
        self.ATW = sb("ATW", [128, NCH * 128], BF16)
        self.BTW = sb("BTW", [128, D], BF16)
        self.PBS = {name: sb(f"PBS_{name}", [128, NCH], F32) for name in self.pb}
        self.WTB = sb("WTB", [8, T], F32)
        self.W8S = sb("W8S", [128, 48], F32)

        self.PD = [es.enter_context(nc.psum_tensor(f"PD{i}", [128, 1024], F32)) for i in range(4)]
        self.PDB = self.PD[3].bitcast(BF16)
        self.bank_state = [[] for _ in range(8)]

        self.P = Prog(nc, es, N_DMA_CH)

        self.xb_consumed = None
        self.xb_free = [None, None]
        self.next_x_ready = [None] * NCH
        self.buf_gate = {}
        self.wsub_busy = [None] * 8
        self.evb_busy = [None] * 8
        self.scr_gate = {}
        self.taps_on = False
        self.tap_toks = []
        self._tapn = 0
        self._tapnames = set()

    def bank(self, i):
        return self.PD[i // 2][:, (i % 2) * 512:(i % 2 + 1) * 512]

    def bank_row0(self, i):
        return self.PD[i // 2][0:1, (i % 2) * 512:(i % 2 + 1) * 512]

    def acq(self, banks, *extra):
        deps = []
        for b in banks:
            deps += self.bank_state[b]
        for e in extra:
            if e is None:
                continue
            deps += e if isinstance(e, list) else [e]
        return deps

    def rel(self, banks, toks):
        toks = toks if isinstance(toks, list) else [toks]
        for b in banks:
            self.bank_state[b] = [t for t in toks if t is not None]

    def arb(self, off, cols):
        return self.AR[:, off:off + cols]

    def arf(self, off, cols):
        return self.ARF[:, off // 2:off // 2 + cols]

    def wcf(self, i):
        return self.WCB[i // 4][:, (i % 4) * 2048:(i % 4) * 2048 + 2048]

    def load_wcf(self, wsrc, col0, i, extra=()):
        deps = list(extra)
        for j in (i, i ^ 4):
            if self.wsub_busy[j] is not None:
                deps.append(self.wsub_busy[j])
        dst = self.wcf(i)

        def fn(g, wsrc=wsrc, col0=col0, dst=dst):
            return g.dma_start(out=dst.rearrange("p (k f) -> p k f", f=128),
                               in_=wsrc[:, col0:col0 + 128].rearrange("(k p) f -> p k f", p=128))
        ch = (CH_W0 + i % 2) if i % 4 < 2 else (CH_W2 + i % 2)
        return self.P.dma("gpsimd", ch, fn, deps=deps)

    def load_wcb(self, wsrc, col0, wblk, slot, extra=()):
        deps = list(extra)
        for i in range(8):
            if self.wsub_busy[i] is not None:
                deps.append(self.wsub_busy[i])
        dst = self.WCB[slot][:, 0:NCH * wblk]

        def fn(g, wsrc=wsrc, col0=col0, wblk=wblk, dst=dst):
            return g.dma_start(out=dst.rearrange("p (k f) -> p k f", f=wblk),
                               in_=wsrc[:, col0:col0 + wblk].rearrange("(k p) f -> p k f", p=128))
        return self.P.dma("gpsimd", CH_W0 + slot, fn, deps=deps)

    def set_wcb_busy(self, slot, tok):
        for i in range(slot * 4, slot * 4 + 4):
            self.wsub_busy[i] = tok

    def evb(self, slot):
        return self.arb(slot * 512, 512)

    def tap(self, name, ap, deps, dtype=None):
        if not self.taps_on:
            return
        shape = list(ap.shape)
        dtyp = dtype or ap.dtype
        dten = self.nc.dram_tensor(f"tap_{name}_{self._tapn}" if name in self._tapnames else f"tap_{name}",
                                   shape, dtyp, kind="ExternalOutput").ap()
        self._tapnames.add(name)
        self._tapn += 1
        tok = self.P.dma("sync", CH_STAT, lambda s, dten=dten, ap=ap: s.dma_start(out=dten[:, :], in_=ap),
                         deps=deps + (self.scr_gate.get("scr_stat") or []))
        self.tap_toks.append(tok)
        self.scr_gate.setdefault("taps", []).append(tok)
        return tok

    def emit_all(self):
        self.load_consts()
        self.final_writes = []
        stages = ["a", "f", "b", "f"][: self.n_stages]
        for b in range(NB):
            xsrc = self.xin[b]
            self.next_x_ready = [None] * NCH
            for si, kind in enumerate(stages):
                last = si == len(stages) - 1
                xdst = self.y[b] if last else self.xs[si % 2]
                if kind == "a":
                    self.emit_attn_a(xsrc, xdst)
                elif kind == "f":
                    self.emit_ffn(xsrc, xdst)
                else:
                    self.emit_attn_b(xsrc, xdst)
                xsrc = xdst
        self.P.op("sync", lambda s: s.nop(), deps=[tmax(*self.final_writes, *self.tap_toks)], inc=False)

    def load_consts(self):
        P = self.P
        pairs = [
            (self.EYEB, self.c_eye_b), (self.EYEF, self.c_eye_f), (self.DIF, self.c_dif),
            (self.MASK, self.c_mask), (self.ROWIDX, self.c_rowidx), (self.COLM, self.c_col128m),
            (self.ONESC, self.c_onescol), (self.ALLONES, self.c_allones),
            (self.RWT, self.rwt), (self.ATW, self.atw), (self.BTW, self.btw),
        ]
        t_all = None
        for dst, src in pairs:
            t_all = P.dma("gpsimd", CH_SMALL, lambda g, dst=dst, src=src: g.dma_start(out=dst[:], in_=src[:, :]))
        for name in self.pb:
            t_all = P.dma("gpsimd", CH_SMALL,
                          lambda g, d=self.PBS[name], s=self.pb[name]: g.dma_start(out=d[:], in_=s[:, :]))
        t_all = P.dma("gpsimd", CH_SMALL,
                      lambda g: g.dma_start(out=self.SLP[:, 0:16], in_=self.slopes.broadcast_to([128, H])))
        t_neg = P.op("vector", lambda v: v.tensor_scalar_mul(self.SLP[:, 16:32], self.SLP[:, 0:16], -1.0),
                     deps=[t_all])
        t_bd = P.op("scalar", lambda s: s.activation(self.SLP[:, 32:48], self.SLP[:, 0:16], AF.Exp, scale=-128.0),
                    deps=[t_all])
        out_toks = [t_neg, t_bd]
        for h in range(H):
            out_toks.append(P.op("scalar", lambda s, h=h: s.activation(
                self.SLP[:, 48 + h:49 + h], self.COLM[:, 0:1], AF.Exp,
                scale=self.SLP[:, 16 + h:17 + h]), deps=[t_neg, t_all]))
        tmp = self.arf(0, 128)
        prev = None
        for h in range(H):
            t1 = P.op("scalar", lambda s, h=h: s.activation(
                tmp, self.DIF[:, :], AF.Exp, scale=self.SLP[:, 0 + h:1 + h]),
                deps=[t_all] + ([prev] if prev else []))
            t2 = P.op("vector", lambda v, h=h: v.tensor_mul(
                self.INTRA[:, h * 128:(h + 1) * 128], tmp, self.MASK[:, :]),
                deps=[t1, t_all])
            prev = t2
            out_toks.append(t2)
        for h in range(H):
            out_toks.append(P.op("scalar", lambda s, h=h: s.activation(
                self.QDECF[:, h * 128:(h + 1) * 128], self.ROWIDX[:, :], AF.Exp,
                scale=self.SLP[:, 16 + h:17 + h]), deps=[t_neg, t_all]))
        self.const_toks = tmax(t_all, *out_toks)
        self.xb_consumed = tmax(*out_toks)

    def emit_stats_cast(self, xsrc):
        P = self.P
        xc = []
        for c in range(NCH):
            deps = [self.xb_consumed, self.next_x_ready[c]]
            xc.append(P.dma("gpsimd", CH_XC + c, lambda g, c=c, xsrc=xsrc: g.dma_start(
                out=self.XB[:, c * T:(c + 1) * T], in_=xsrc[c * 128:(c + 1) * 128, :]), deps=deps))
        self.xsrc_reads = list(xc)
        deps0 = self.acq([4, 5, 6, 7], [self.const_toks])
        last1 = {}
        for c in range(NCH):
            for tb in range(4):
                def mm(t, c=c, tb=tb):
                    return t.matmul(self.bank_row0(4 + tb), self.ONESC[:, 0:1],
                                    self.XB[:, c * T + tb * 512: c * T + tb * 512 + 512],
                                    start=(c == 0), stop=(c == NCH - 1))
                last1[tb] = P.op("tensor", mm, deps=deps0 + [xc[c]], inc=(c == NCH - 1))
        mtok = []
        for tb in range(4):
            ev = P.op("vector", lambda v, tb=tb: v.tensor_scalar_mul(
                self.STV[0:1, tb * 512:(tb + 1) * 512], self.bank_row0(4 + tb), 1.0 / D), deps=[last1[tb]])
            mtok.append(ev)
            self.rel([4 + tb], [ev])
        deps2 = self.acq([4, 5, 6, 7])
        sq_free = [None, None]
        last2 = {}
        for c in range(NCH):
            slot = c % 2
            sq = self.arb(slot * 2048, 2048)
            sdeps = [xc[c], self.xb_consumed]
            if sq_free[slot] is not None:
                sdeps.append(sq_free[slot])
            tsq = P.op("scalar", lambda s, c=c, sq=sq: s.activation(
                sq, self.XB[:, c * T:(c + 1) * T], AF.Square), deps=sdeps)
            for tb in range(4):
                def mm(t, sq=sq, tb=tb, c=c):
                    return t.matmul(self.bank_row0(4 + tb), self.ONESC[:, 0:1],
                                    sq[:, tb * 512: tb * 512 + 512],
                                    start=(c == 0), stop=(c == NCH - 1))
                last2[tb] = P.op("tensor", mm, deps=deps2 + [tsq], inc=(c == NCH - 1 or tb == 3))
            sq_free[slot] = last2[3]
        qtok = []
        for tb in range(4):
            ev = P.op("vector", lambda v, tb=tb: v.tensor_scalar_mul(
                self.STV[0:1, T + tb * 512: T + (tb + 1) * 512], self.bank_row0(4 + tb), 1.0 / D),
                deps=[last2[tb]])
            qtok.append(ev)
            self.rel([4 + tb], [ev])
        self.sq_mm_done = qtok
        ftok, ntok = [], []
        for tb in range(4):
            m = self.STV[0:1, tb * 512:(tb + 1) * 512]
            q = self.STV[0:1, T + tb * 512: T + (tb + 1) * 512]
            iv = self.STV[0:1, 2 * T + tb * 512: 2 * T + (tb + 1) * 512]
            t1 = P.op("vector", lambda v, m=m, iv=iv: v.tensor_mul(iv, m, m), deps=[mtok[tb], qtok[tb]])
            t2 = P.op("vector", lambda v, q=q, iv=iv: v.tensor_sub(q, q, iv), deps=[t1])
            t3 = P.op("vector", lambda v, q=q: v.tensor_scalar_add(q, q, EPS), deps=[t2])
            t4 = P.op("vector", lambda v, q=q, iv=iv: v.reciprocal(iv, q), deps=[t3])
            t5 = P.op("scalar", lambda s, iv=iv: s.activation(iv, iv, AF.Sqrt), deps=[t4])
            tn = P.op("vector", lambda v, m=m: v.tensor_scalar_mul(m, m, -1.0), deps=[t1])
            ftok.append(t5)
            ntok.append(tn)
        g = self.scr_gate.get("scr_stat") or []
        tw = P.dma("sync", CH_STAT, lambda s: s.dma_start(out=self.scr_stat[0:1, 0:T],
                                                          in_=self.STV[0:1, 0:T]), deps=ntok + g)
        NMB = self.arf(4096, 2048)
        INVB = self.arf(8192, 2048)
        tb1 = P.dma("sync", CH_R0, lambda s: s.dma_start(
            out=NMB, in_=self.scr_stat[0:1, 0:T].broadcast_to([128, T])), deps=[tw, self.xb_consumed])
        tw2 = P.dma("sync", CH_STAT, lambda s: s.dma_start(out=self.scr_stat[0:1, T:2 * T],
                                                           in_=self.STV[0:1, 2 * T:3 * T]), deps=ftok)
        tb2 = P.dma("sync", CH_R1, lambda s: s.dma_start(
            out=INVB, in_=self.scr_stat[0:1, T:2 * T].broadcast_to([128, T])), deps=[tw2, self.xb_consumed])
        self.scr_gate["scr_stat"] = [tb1, tb2]
        xnb = []
        prevv = None
        for c in range(NCH):
            eng = "gpsimd" if c % 3 == 2 else "vector"
            off = 12288 if eng == "vector" else 14336
            TM = self.arf(off, 1024)
            d1 = [xc[c], tb1, self.xb_consumed]
            if eng == "vector" and prevv is not None:
                d1.append(prevv)
            t1a = P.op(eng, lambda v, c=c, TM=TM: v.tensor_add(TM, self.XB[:, c * T:c * T + 1024], NMB[:, 0:1024]),
                       deps=d1)
            t1b = P.op(eng, lambda v, c=c, TM=TM: v.tensor_mul(self.XB[:, c * T:c * T + 1024], TM, INVB[:, 0:1024]),
                       deps=[t1a, tb2])
            t2a = P.op(eng, lambda v, c=c, TM=TM: v.tensor_add(TM, self.XB[:, c * T + 1024:(c + 1) * T],
                                                               NMB[:, 1024:2048]), deps=[t1b])
            t2b = P.op(eng, lambda v, c=c, TM=TM: v.tensor_mul(self.XB[:, c * T + 1024:(c + 1) * T], TM,
                                                               INVB[:, 1024:2048]), deps=[t2a])
            if eng == "vector":
                prevv = t2b
            xnb.append(tmax(t1b, t2b))
        self.xnb_tok = xnb
        self.xnb_all = tmax(*xnb)
        self.stat_busy = [tb1, tb2]
        return xnb

    def proj_feature(self, wsrc, wcol0, writer, n_out=NCH):
        P = self.P
        wtok = {}
        for d0 in range(min(4, n_out)):
            wtok[d0] = self.load_wcf(wsrc, wcol0 + d0 * 128, d0 % 8)
        for do in range(n_out):
            sub = do % 8
            banks = [(do % 2) * 4 + tb for tb in range(4)]
            gdeps = self.acq(banks, [wtok[do]])
            last = None
            for k in range(NCH):
                for tb in range(4):
                    def mm(t, k=k, tb=tb, bk=banks[tb], sub=sub):
                        return t.matmul(self.bank(bk),
                                        self.wcf(sub)[:, k * 128:(k + 1) * 128],
                                        self.XB[:, k * T + tb * 512: k * T + tb * 512 + 512],
                                        start=(k == 0), stop=(k == NCH - 1))
                    last = P.op("tensor", mm, deps=(gdeps + [self.xnb_tok[k]]) if tb == 0 else (),
                                inc=(k == NCH - 1 and tb == 3))
            self.wsub_busy[sub] = last
            if do + 4 < n_out:
                wtok[do + 4] = self.load_wcf(wsrc, wcol0 + (do + 4) * 128, (do + 4) % 8)
            for tb in range(4):
                cons = writer(do, tb, banks[tb], [last])
                self.rel([banks[tb]], cons)

    def proj_token(self, wsrc, wcol0, wcols, writer):
        P = self.P
        n_dob = wcols // 512
        wtok = {0: self.load_wcb(wsrc, wcol0, 512, 0)}
        if n_dob > 1:
            wtok[1] = self.load_wcb(wsrc, wcol0 + 512, 512, 1)
        for dob in range(n_dob):
            slot = dob % 2
            last = None
            for tc in range(NCH):
                bk = (tc % 2) * 4 + dob
                gdeps = self.acq([bk], [wtok[dob]])
                for k in range(NCH):
                    def mm(t, k=k, tc=tc, bk=bk, slot=slot):
                        return t.matmul(self.bank(bk),
                                        self.XB[:, k * T + tc * 128: k * T + tc * 128 + 128],
                                        self.WCB[slot][:, k * 512:(k + 1) * 512],
                                        start=(k == 0), stop=(k == NCH - 1))
                    last = P.op("tensor", mm,
                                deps=(gdeps + [self.xnb_tok[k]]) if k == 0 else [self.xnb_tok[k]],
                                inc=(k == NCH - 1))
                cons = writer(tc, dob, bk, [last])
                self.rel([bk], cons)
            self.set_wcb_busy(slot, last)
            if dob + 2 < n_dob:
                wtok[dob + 2] = self.load_wcb(wsrc, wcol0 + (dob + 2) * 512, 512, slot)

    def feat_writer(self, dst_fn, bias_name, scr_key):
        P = self.P

        def writer(do, tb, bk, deps):
            slot = (do * 4 + tb) % 4
            ev = self.evb(slot)
            d = list(deps)
            if self.evb_busy[slot] is not None:
                d.append(self.evb_busy[slot])
            bias = self.PBS[bias_name][:, do:do + 1]
            tok = P.op("vector", lambda v, ev=ev, bk=bk, bias=bias: v.tensor_scalar(
                ev, self.bank(bk), bias, None, ALU.add), deps=d)
            wd = [tok] + (self.scr_gate.get(scr_key) or [])

            def dfn(s, do=do, tb=tb, ev=ev):
                o, i = dst_fn(do, tb, ev)
                return s.dma_start(out=o, in_=i)
            wtok = P.dma("sync", CH_EV + slot, dfn, deps=wd)
            self.evb_busy[slot] = wtok
            self.cur_scr_writes.setdefault(scr_key, []).append(wtok)
            return [tok]
        return writer

    def tok_writer(self, dst_fn, bvb_ap, bvb_tok, scr_key):
        P = self.P

        def writer(tc, dob, bk, deps):
            slot = (tc * 4 + dob) % 4 + 4
            ev = self.evb(slot)
            d = list(deps) + [bvb_tok]
            if self.evb_busy[slot] is not None:
                d.append(self.evb_busy[slot])
            tok = P.op("vector", lambda v, ev=ev, bk=bk, dob=dob: v.tensor_add(
                ev, self.bank(bk), bvb_ap[:, dob * 512:(dob + 1) * 512]), deps=d)
            wd = [tok] + (self.scr_gate.get(scr_key) or [])

            def dfn(s, tc=tc, dob=dob, ev=ev):
                o, i = dst_fn(tc, dob, ev)
                return s.dma_start(out=o, in_=i)
            wtok = P.dma("sync", CH_EV + slot, dfn, deps=wd)
            self.evb_busy[slot] = wtok
            self.cur_scr_writes.setdefault(scr_key, []).append(wtok)
            return [tok]
        return writer

    def residual_writer(self, xsrc, xdst, bias_name):
        P = self.P
        state = {"xo": [None, None], "ev": [None, None]}

        def writer(do, tbg, bk, deps):
            xslot = (do * 4 + tbg) % 2
            XOLD = self.arf(12288 + xslot * 1024, 512)
            EVF = self.arf(14336 + xslot * 1024, 512)
            xod = [state["xo"][xslot], self.next_x_ready[do], self.xnb_all]
            txo = P.dma("sync", CH_XO0 + xslot, lambda s, do=do, tbg=tbg, XOLD=XOLD: s.dma_start(
                out=XOLD, in_=xsrc[do * 128:(do + 1) * 128, tbg * 512:(tbg + 1) * 512]), deps=xod)
            self.xsrc_reads.append(txo)
            evd = [state["ev"][xslot], txo, self.xnb_all] + list(deps)
            if bias_name is not None:
                bias = self.PBS[bias_name][:, do:do + 1]
                tev = P.op("vector", lambda v, bk=bk, XOLD=XOLD, EVF=EVF, bias=bias: v.scalar_tensor_tensor(
                    EVF, self.bank(bk), bias, XOLD, ALU.add, ALU.add), deps=evd)
            else:
                tev = P.op("vector", lambda v, bk=bk, XOLD=XOLD, EVF=EVF: v.tensor_add(
                    EVF, self.bank(bk), XOLD), deps=evd)
            wd = [tev] + (self.buf_gate.get(id(xdst)) or [])
            twr = P.dma("sync", CH_XW + do, lambda s, do=do, tbg=tbg, EVF=EVF: s.dma_start(
                out=xdst[do * 128:(do + 1) * 128, tbg * 512:(tbg + 1) * 512], in_=EVF), deps=wd)
            state["xo"][xslot] = tev
            state["ev"][xslot] = twr
            self.cur_x_writes[(do, tbg)] = twr
            return [tev]
        return writer

    def begin_stage(self):
        self.cur_scr_writes = {}
        self.cur_x_writes = {}
        self.xsrc_reads = []

    def finish_stage(self, xsrc, xdst, extra_reads, scr_keys_read, all_done_toks):
        for k in scr_keys_read:
            self.scr_gate[k] = list(tmax(*extra_reads))
        self.buf_gate[id(xsrc)] = list(tmax(*self.xsrc_reads))
        self.next_x_ready = [tmax(*(self.cur_x_writes.get((c, tb)) for tb in range(4))) for c in range(NCH)]
        self.xb_consumed = tmax(*all_done_toks)
        self.final_writes += list(self.cur_x_writes.values())
        self.buf_gate.pop(id(xdst), None)

    def emit_attn_a(self, xsrc, xdst):
        P = self.P
        self.begin_stage()
        self.emit_stats_cast(xsrc)
        self.proj_feature(self.wk, 0, self.feat_writer(
            lambda do, tb, ev: (self.a_k[do, :, tb * 512:(tb + 1) * 512], ev), "bk", "a_k"))
        self.proj_feature(self.wq, 0, self.feat_writer(
            lambda do, tb, ev: (self.a_q[do, :, tb * 512:(tb + 1) * 512], ev), "bq", "a_q"))
        BVB = self.arf(4096, 2048)
        bv_tok = P.dma("sync", CH_R2, lambda s: s.dma_start(
            out=BVB, in_=self.row_bv.broadcast_to([128, D])), deps=self.xnb_tok + [self.stat_busy[0]])

        def av_dst(tc, dob, ev):
            return (self.a_v[dob * 4:(dob + 1) * 4, :, tc, :].rearrange("h p f -> p h f"),
                    ev.rearrange("p (h f) -> p h f", f=128))
        self.proj_token(self.wv, 0, D, self.tok_writer(av_dst, BVB, bv_tok, "a_v"))
        aq_all = tmax(*self.cur_scr_writes["a_q"])
        ak_all = tmax(*self.cur_scr_writes["a_k"])
        av_all = tmax(*self.cur_scr_writes["a_v"])

        KT = lambda s: self.arb(0 + s * 2048, 2048)
        VH = lambda s: self.arb(4096 + s * 2048, 2048)
        QT = lambda s: self.arb(8192 + s * 1024, 1024)
        RB = self.arf(10240, 1024)
        expST = lambda kt: self.XB[:, kt * 1024:(kt + 1) * 1024]
        PREF = lambda h: self.XB[:, 16384 + h * 1024: 16384 + (h + 1) * 1024]
        phase_gate = tmax(aq_all, ak_all, av_all, self.xnb_all, *self.stat_busy, bv_tok)

        kt_busy = [None, None]
        vh_busy = [None, None]
        qt_busy = [None, None]
        exp_free = {}
        pref_gate = {}
        a3_reads = []
        res_writer = self.residual_writer(xsrc, xdst, "bo")
        for qbp in range(2):
            for h in range(H):
                slot = h % 2
                tk = P.dma("sync", CH_R0 + slot, lambda s, h=h, slot=slot: s.dma_start(
                    out=KT(slot)[:, :], in_=self.a_k[h, :, :]),
                    deps=[kt_busy[slot], phase_gate])
                tv = P.dma("sync", CH_R2 + slot, lambda s, h=h, slot=slot: s.dma_start(
                    out=VH(slot)[:, :].rearrange("p (kt f) -> p kt f", f=128),
                    in_=self.a_v[h]), deps=[vh_busy[slot], phase_gate])
                tq = P.dma("sync", CH_R4 + slot, lambda s, h=h, qbp=qbp, slot=slot: s.dma_start(
                    out=QT(slot)[:, :], in_=self.a_q[h, :, qbp * 1024:(qbp + 1) * 1024]),
                    deps=[qt_busy[slot], phase_gate])
                a3_reads += [tk, tv, tq]
                exp_toks = {}
                last_st = None
                for kt in range(NCH):
                    pdi = kt % 2
                    deps = self.acq([pdi * 2, pdi * 2 + 1], [tk, tq])
                    P.op("tensor", lambda t, kt=kt, pdi=pdi, slot=slot: t.matmul(
                        self.PD[pdi][:, 0:512], KT(slot)[:, kt * 128:(kt + 1) * 128],
                        QT(slot)[:, 0:512], start=True, stop=True), deps=deps, inc=False)
                    m1 = P.op("tensor", lambda t, kt=kt, pdi=pdi, slot=slot: t.matmul(
                        self.PD[pdi][:, 512:1024], KT(slot)[:, kt * 128:(kt + 1) * 128],
                        QT(slot)[:, 512:1024], start=True, stop=True))
                    te = P.op("scalar", lambda s, kt=kt, pdi=pdi: s.activation(
                        expST(kt), self.PD[pdi][:, :], AF.Exp, scale=1.0 / 128.0),
                        deps=[m1, exp_free.get(kt), phase_gate if (qbp == 0 and h == 0) else None])
                    exp_toks[kt] = te
                    last_st = m1
                    self.rel([pdi * 2, pdi * 2 + 1], [te])
                av0 = self.acq([4], [tv])
                av1 = self.acq([5])
                sm = self.acq([6, 7])
                last_av = None
                for kt in range(NCH):
                    P.op("tensor", lambda t, kt=kt, slot=slot: t.matmul(
                        self.PD[2][:, 0:512], VH(slot)[:, kt * 128:(kt + 1) * 128],
                        expST(kt)[:, 0:512], start=(kt == 0), stop=(kt == NCH - 1)),
                        deps=(av0 if kt == 0 else []) + [exp_toks[kt]], inc=False)
                    P.op("tensor", lambda t, kt=kt, slot=slot: t.matmul(
                        self.PD[2][:, 512:1024], VH(slot)[:, kt * 128:(kt + 1) * 128],
                        expST(kt)[:, 512:1024], start=(kt == 0), stop=(kt == NCH - 1)),
                        deps=av1 if kt == 0 else (), inc=False)
                    P.op("tensor", lambda t, kt=kt: t.matmul(
                        self.PD[3][:, 0:512], self.ALLONES[:, :], expST(kt)[:, 0:512],
                        start=(kt == 0), stop=(kt == NCH - 1)), deps=sm if kt == 0 else (), inc=False)
                    s1 = P.op("tensor", lambda t, kt=kt: t.matmul(
                        self.PD[3][:, 512:1024], self.ALLONES[:, :], expST(kt)[:, 512:1024],
                        start=(kt == 0), stop=(kt == NCH - 1)))
                    exp_free[kt] = s1
                    last_av = s1
                r0 = P.op("vector", lambda v: v.reciprocal(RB[:, 0:512], self.PD[3][:, 0:512]),
                          deps=[last_av, phase_gate if (qbp == 0 and h == 0) else None])
                r1 = P.op("vector", lambda v: v.reciprocal(RB[:, 512:1024], self.PD[3][:, 512:1024]),
                          deps=[r0])
                self.rel([6, 7], [r1])
                e0 = P.op("vector", lambda v, h=h: v.tensor_mul(
                    PREF(h)[:, 0:512], self.PD[2][:, 0:512], RB[:, 0:512]),
                    deps=[last_av, r0] + (pref_gate.get(h) or []))
                e1 = P.op("vector", lambda v, h=h: v.tensor_mul(
                    PREF(h)[:, 512:1024], self.PD[2][:, 512:1024], RB[:, 512:1024]), deps=[r1])
                self.rel([4], [e0])
                self.rel([5], [e1])
                kt_busy[slot] = last_st
                qt_busy[slot] = last_st
                vh_busy[slot] = last_av
                pref_gate[h] = [e0, e1]
            wtok = {d0: self.load_wcf(self.wo, d0 * 128, d0 % 8) for d0 in range(4)}
            new_pref_gate = {}
            for do in range(NCH):
                sub = do % 8
                banks = [(do % 2) * 4 + qb for qb in range(2)]
                gdeps = self.acq(banks, [wtok[do]])
                for h in range(H):
                    gdeps += pref_gate.get(h) or []
                last = None
                for k in range(NCH):
                    for qb in range(2):
                        def mm(t, k=k, qb=qb, bk=banks[qb], sub=sub):
                            return t.matmul(self.bank(bk),
                                            self.wcf(sub)[:, k * 128:(k + 1) * 128],
                                            PREF(k)[:, qb * 512:(qb + 1) * 512],
                                            start=(k == 0), stop=(k == NCH - 1))
                        last = P.op("tensor", mm, deps=gdeps if (k == 0 and qb == 0) else (),
                                    inc=(k == NCH - 1 and qb == 1))
                self.wsub_busy[sub] = last
                for h in range(H):
                    new_pref_gate.setdefault(h, []).append(last)
                if do + 4 < NCH:
                    wtok[do + 4] = self.load_wcf(self.wo, (do + 4) * 128, (do + 4) % 8)
                for qb in range(2):
                    cons = res_writer(do, qbp * 2 + qb, banks[qb], [last])
                    self.rel([banks[qb]], cons)
            pref_gate = new_pref_gate
        all_done = [tmax(*(pref_gate[h] for h in range(H)), *self.cur_x_writes.values())]
        self.finish_stage(xsrc, xdst, a3_reads, ["a_q", "a_k", "a_v"], all_done)

    def emit_ffn(self, xsrc, xdst):
        P = self.P
        self.begin_stage()
        self.emit_stats_cast(xsrc)
        lg_evac = []
        for tb in range(4):
            bk = 4 + tb % 2
            gdeps = self.acq([bk])
            last = None
            for k in range(NCH):
                def mm(t, k=k, tb=tb, bk=bk):
                    return t.matmul(self.PD[bk // 2][0:8, (bk % 2) * 512:(bk % 2) * 512 + 512],
                                    self.RWT[:, k * 8:(k + 1) * 8],
                                    self.XB[:, k * T + tb * 512: k * T + tb * 512 + 512],
                                    start=(k == 0), stop=(k == NCH - 1))
                last = P.op("tensor", mm, deps=gdeps + [self.xnb_tok[k]], inc=(k == NCH - 1))
            ev = P.op("vector", lambda v, tb=tb, bk=bk: v.tensor_copy(
                self.arf(0, 2048)[0:8, tb * 512:(tb + 1) * 512],
                self.PD[bk // 2][0:8, (bk % 2) * 512:(bk % 2) * 512 + 512]),
                deps=[last] + self.sq_mm_done)
            self.rel([bk], [ev])
            lg_evac.append(ev)
        LGTF = self.arf(0, 2048)
        self.tap("lgt", LGTF[0:8, :], lg_evac)
        AXW = self.arb(8192, 2048)
        axt_last = {}
        for tb in range(4):
            bk = 4 + tb % 2
            gdeps = self.acq([bk])
            last = None
            for k in range(NCH):
                def mm(t, k=k, tb=tb, bk=bk):
                    return t.matmul(self.bank(bk), self.ATW[:, k * 128:(k + 1) * 128],
                                    self.XB[:, k * T + tb * 512: k * T + tb * 512 + 512],
                                    start=(k == 0), stop=(k == NCH - 1))
                last = P.op("tensor", mm, deps=gdeps + [self.xnb_tok[k]], inc=(k == NCH - 1))
            axt_last[tb] = last
            tx = P.op("vector", lambda v, tb=tb, bk=bk: v.tensor_scalar_mul(
                AXW[:, tb * 512:(tb + 1) * 512], self.bank(bk), 2.0),
                deps=[last, self.xnb_all, self.stat_busy[1]])
            self.rel([bk], [tx])
            axt_last[tb] = tx
        w8 = self.W8S
        w8_done = []
        for ti in range(NCH):
            tb = ti // 4
            d0 = self.acq([6], [lg_evac[tb]])
            t_tr = P.op("tensor", lambda t, ti=ti: t.transpose(
                self.PD[3][:, 0:8], LGTF[0:8, ti * 128:(ti + 1) * 128], self.EYEF[0:8, 0:8]), deps=d0)
            Lp = self.PD[3][:, 0:8]
            Lc = w8[:, 0:8]
            m1, m2, rc = w8[:, 40:41], w8[:, 41:42], w8[:, 42:43]
            E, Msk, Ew, W8 = w8[:, 8:16], w8[:, 16:24], w8[:, 24:32], w8[:, 32:40]
            v0 = P.op("vector", lambda v: v.tensor_copy(Lc, Lp),
                      deps=[t_tr, self.xnb_all] if ti == 0 else [t_tr])
            self.rel([6], [v0])
            s1 = P.op("scalar", lambda s: s.activation(E, Lc, AF.Exp), deps=[v0])
            v1 = P.op("vector", lambda v: v.reduce_max(m1, Lc, axis=AX.X), deps=[v0], selfwait=True)
            v3 = P.op("vector", lambda v: v.tensor_scalar(Msk, Lc, m1, -1e30, ALU.is_ge, ALU.mult),
                      deps=[v1], selfwait=True)
            v4 = P.op("vector", lambda v: v.tensor_tensor(Ew, Lc, Msk, ALU.add), deps=[v3], selfwait=True)
            v5 = P.op("vector", lambda v: v.reduce_max(m2, Ew, axis=AX.X), deps=[v4], selfwait=True)
            v6 = P.op("vector", lambda v: v.tensor_scalar(Msk, Lc, m2, 1.0, ALU.is_ge, ALU.mult),
                      deps=[v5, s1], selfwait=True)
            v7 = P.op("vector", lambda v: v.tensor_mul(Ew, E, Msk), deps=[v6], selfwait=True)
            v8 = P.op("vector", lambda v: v.tensor_reduce(rc, Ew, axis=AX.X, op=ALU.add),
                      deps=[v7], selfwait=True)
            v9 = P.op("vector", lambda v: v.reciprocal(rc, rc), deps=[v8], selfwait=True)
            va = P.op("vector", lambda v: v.tensor_scalar(W8, Ew, rc, None, ALU.mult),
                      deps=[v9], selfwait=True)
            d1 = self.acq([7])
            t_tw = P.op("tensor", lambda t: t.transpose(self.PD[3][0:8, 512:640], W8, self.EYEF[:, :]),
                        deps=d1 + [va])
            vb = P.op("vector", lambda v, ti=ti: v.tensor_copy(
                self.WTB[0:8, ti * 128:(ti + 1) * 128], self.PD[3][0:8, 512:640]), deps=[t_tw])
            self.rel([7], [vb])
            w8_done.append(vb)
            if ti == 0 and self.taps_on:
                ttok = self.tap("w8all", w8[:, 0:48], [va, s1])
                self.bank_state[6] = self.bank_state[6] + [ttok]
        WB = self.arf(4096, 2048)
        gw = self.scr_gate.get("scr_w8") or []
        t_w = P.dma("sync", CH_STAT, lambda s: s.dma_start(out=self.scr_w8[:, :], in_=self.WTB[:, :]),
                    deps=w8_done + gw)
        wb_toks = []
        for e in range(8):
            wb_toks.append(P.dma("sync", CH_R3, lambda s, e=e: s.dma_start(
                out=WB[e * 16:(e + 1) * 16, :],
                in_=self.scr_w8[e:e + 1, :].broadcast_to([16, T])),
                deps=[t_w, self.xnb_all, self.stat_busy[0]]))
        wb_all = tmax(*wb_toks)
        self.scr_gate["scr_w8"] = wb_toks
        self.tap("wtb", self.WTB[:, :], w8_done)
        self.tap("wb", WB[:, :], wb_toks)
        axw_toks = []
        for tb in range(4):
            tx = P.op("vector", lambda v, tb=tb: v.tensor_mul(
                AXW[:, tb * 512:(tb + 1) * 512], AXW[:, tb * 512:(tb + 1) * 512],
                WB[:, tb * 512:(tb + 1) * 512]),
                deps=[axt_last[tb], wb_all])
            axw_toks.append(tx)
        self.tap("axw", AXW[:, :], axw_toks)
        res_writer = self.residual_writer(xsrc, xdst, None)
        wtok = {d0: self.load_wcf(self.wft, d0 * 128, d0 % 8) for d0 in range(4)}
        for do in range(NCH):
            sub = do % 8
            banks = [(do % 2) * 4 + tb for tb in range(4)]
            gdeps = self.acq(banks, [wtok[do]])
            for k in range(NCH):
                for tb in range(4):
                    def mm(t, k=k, tb=tb, bk=banks[tb], sub=sub):
                        return t.matmul(self.bank(bk),
                                        self.wcf(sub)[:, k * 128:(k + 1) * 128],
                                        self.XB[:, k * T + tb * 512: k * T + tb * 512 + 512],
                                        start=(k == 0), stop=False)
                    P.op("tensor", mm, deps=(gdeps + [self.xnb_tok[k]]) if tb == 0 else (), inc=False)
            last = None
            for tb in range(4):
                def mmbt(t, tb=tb, bk=banks[tb], do=do):
                    return t.matmul(self.bank(bk), self.BTW[:, do * 128:(do + 1) * 128],
                                    AXW[:, tb * 512:(tb + 1) * 512], start=False, stop=True)
                last = P.op("tensor", mmbt, deps=[axw_toks[tb]], inc=(tb == 3))
            self.wsub_busy[sub] = last
            if do + 4 < NCH:
                wtok[do + 4] = self.load_wcf(self.wft, (do + 4) * 128, (do + 4) % 8)
            for tb in range(4):
                cons = res_writer(do, tb, banks[tb], [last])
                self.rel([banks[tb]], cons)
        all_done = [tmax(*self.cur_x_writes.values())]
        self.finish_stage(xsrc, xdst, [wb_all], [], all_done)

    def emit_attn_b(self, xsrc, xdst):
        P = self.P
        self.begin_stage()
        self.emit_stats_cast(xsrc)
        self.proj_feature(self.wqkv, 0, self.feat_writer(
            lambda do, tb, ev: (self.b_q[tb * 4:(tb + 1) * 4, :, do, :].rearrange("c p f -> p c f"),
                                ev.rearrange("p (c f) -> p c f", f=128)), "bqkv_q", "b_q"))
        self.proj_feature(self.wqkv, D, self.feat_writer(
            lambda do, tb, ev: (self.b_k[tb * 4:(tb + 1) * 4, :, do, :].rearrange("c p f -> p c f"),
                                ev.rearrange("p (c f) -> p c f", f=128)), "bqkv_k", "b_k"))
        BVB = self.arf(4096, 2048)
        bv_tok = P.dma("sync", CH_R2, lambda s: s.dma_start(
            out=BVB, in_=self.row_bqv.broadcast_to([128, D])), deps=self.xnb_tok + [self.stat_busy[0]])
        self.proj_token(self.wqkv, 2 * D, D, self.tok_writer(
            lambda tc, dob, ev: (self.b_v[tc, :, dob * 512:(dob + 1) * 512], ev), BVB, bv_tok, "b_v"))
        bq_all = tmax(*self.cur_scr_writes["b_q"])
        bk_all = tmax(*self.cur_scr_writes["b_k"])
        bv_all = tmax(*self.cur_scr_writes["b_v"])
        phase_gate = tmax(bq_all, bk_all, bv_all, self.xnb_all, *self.stat_busy, bv_tok)

        PREFB = lambda sl: self.XB[:, sl * 8192:(sl + 1) * 8192]
        S_sb = self.XBF[:, 8192:10240]
        SSB = self.XB[:, 20480:22528]
        QDB = self.XB[:, 22528:24576]
        STB = self.XB[:, 24576:26624]
        KDB = self.XB[:, 26624:28672]
        VC = lambda sl: self.XB[:, 28672 + sl * 2048: 28672 + (sl + 1) * 2048]
        QC = lambda sl: self.arb(0 + sl * 2048, 2048)
        KC = lambda sl: self.arb(4096 + sl * 2048, 2048)

        z0 = P.op("vector", lambda v: v.memset(S_sb[:, :], 0.0), deps=[phase_gate])
        z1 = P.op("vector", lambda v: v.memset(SSB[:, :], 0.0), deps=[phase_gate])
        state_tok = {h: [z0, z1] for h in range(H)}
        qc_busy = [None, None]
        kc_busy = [None, None]
        vc_busy = [None, None]
        stb_gate = {}
        qdb_gate = {}
        kdb_gate = {}
        prefb_gate = [None, None]
        b3_reads = []
        res_writer = self.residual_writer(xsrc, xdst, "bout")
        o_evac = {}
        for c in range(NCH):
            slot = c % 2
            tqc = P.dma("sync", CH_R0 + slot, lambda s, c=c, slot=slot: s.dma_start(
                out=QC(slot)[:, :].rearrange("p (h f) -> p h f", f=128),
                in_=self.b_q[c]), deps=[qc_busy[slot], phase_gate])
            tkc = P.dma("sync", CH_R2 + slot, lambda s, c=c, slot=slot: s.dma_start(
                out=KC(slot)[:, :].rearrange("p (h f) -> p h f", f=128),
                in_=self.b_k[c]), deps=[kc_busy[slot], phase_gate])
            tvc = P.dma("sync", CH_R4 + slot, lambda s, c=c, slot=slot: s.dma_start(
                out=VC(slot)[:, :], in_=self.b_v[c]), deps=[vc_busy[slot], phase_gate])
            b3_reads += [tqc, tkc, tvc]
            chunk_vc_uses = []
            for h in range(H):
                hp = h % 2
                st_b, o_b, u_b = hp, 2 + hp, 4 + hp
                hs = slice(h * 128, (h + 1) * 128)
                m_st = P.op("tensor", lambda t, bk=st_b, hs=hs, slot=slot: t.matmul(
                    self.bank(bk)[:, 0:128], KC(slot)[:, hs], QC(slot)[:, hs],
                    start=True, stop=True), deps=self.acq([st_b], [tqc, tkc]))
                v_dec = P.op("vector", lambda v, bk=st_b, hs=hs: v.tensor_mul(
                    STB[:, hs], self.bank(bk)[:, 0:128], self.INTRA[:, hs]),
                    deps=[m_st, phase_gate if c == 0 else None] + (stb_gate.get(h) or []))
                self.rel([st_b], [v_dec])
                kd_bank = 6 + hp
                kd_ap = self.PDB[:, (kd_bank - 6) * 1024:(kd_bank - 6) * 1024 + 128]
                m_tr = P.op("tensor", lambda t, kd_ap=kd_ap, hs=hs, slot=slot: t.transpose(
                    kd_ap, KC(slot)[:, hs], self.EYEB[:, :]), deps=self.acq([kd_bank], [tkc]))
                s_kd = P.op("scalar", lambda s, kd_ap=kd_ap, hs=hs, h=h: s.mul(
                    KDB[:, hs], kd_ap, self.SLP[:, 48 + h:49 + h]),
                    deps=[m_tr, phase_gate if c == 0 else None] + (kdb_gate.get(h) or []))
                self.rel([kd_bank], [s_kd])
                v_qd = P.op("vector", lambda v, hs=hs, slot=slot: v.tensor_mul(
                    QDB[:, hs], QC(slot)[:, hs], self.QDECF[:, hs]),
                    deps=[tqc, phase_gate if c == 0 else None] + (qdb_gate.get(h) or []))
                m_oi = P.op("tensor", lambda t, bk=o_b, hs=hs, slot=slot: t.matmul(
                    self.bank(bk)[:, 0:128], VC(slot)[:, hs], STB[:, hs],
                    start=True, stop=False), deps=self.acq([o_b], [v_dec, tvc]))
                m_oe = P.op("tensor", lambda t, bk=o_b, hs=hs: t.matmul(
                    self.bank(bk)[:, 0:128], SSB[:, hs], QDB[:, hs],
                    start=False, stop=True), deps=[v_qd] + state_tok[h])
                sl_p = (c // 4) % 2
                s_oe = P.op("scalar", lambda s, bk=o_b, h=h, c=c, sl_p=sl_p: s.copy(
                    PREFB(sl_p)[:, h * 512 + (c % 4) * 128: h * 512 + (c % 4) * 128 + 128],
                    self.bank(bk)[:, 0:128]),
                    deps=[m_oe, prefb_gate[sl_p]])
                self.rel([o_b], [s_oe])
                o_evac.setdefault(c // 4, []).append(s_oe)
                m_up = P.op("tensor", lambda t, bk=u_b, hs=hs, slot=slot: t.matmul(
                    self.bank(bk)[:, 0:128], KDB[:, hs], VC(slot)[:, hs],
                    start=True, stop=True), deps=self.acq([u_b], [s_kd, tvc]))
                v_st = P.op("vector", lambda v, bk=u_b, hs=hs, h=h: v.scalar_tensor_tensor(
                    S_sb[:, hs], S_sb[:, hs], self.SLP[:, 32 + h:33 + h],
                    self.bank(bk)[:, 0:128], ALU.mult, ALU.add), deps=[m_up] + state_tok[h])
                s_cast = P.op("scalar", lambda s, hs=hs: s.copy(SSB[:, hs], S_sb[:, hs]),
                              deps=[v_st, m_oe])
                self.rel([u_b], [v_st])
                state_tok[h] = [s_cast]
                stb_gate[h] = [m_oi]
                qdb_gate[h] = [m_oe]
                kdb_gate[h] = [m_up]
                chunk_vc_uses += [m_oi, m_up]
            qc_busy[slot] = tmax(*(qdb_gate[h] for h in range(H)))
            kc_busy[slot] = tmax(*(kdb_gate[h] for h in range(H)))
            vc_busy[slot] = tmax(*chunk_vc_uses)
            if c % 4 == 3:
                tbg = c // 4
                sl_p = tbg % 2
                wtok = {d0: self.load_wcf(self.wout, d0 * 128, d0 % 8) for d0 in range(4)}
                pf_done = o_evac[tbg]
                gates = []
                for do in range(NCH):
                    sub = do % 8
                    bk = do % 4
                    gdeps = self.acq([bk], [wtok[do]]) + pf_done
                    last = None
                    for k in range(NCH):
                        def mm(t, k=k, bk=bk, sub=sub, sl_p=sl_p):
                            return t.matmul(self.bank(bk),
                                            self.wcf(sub)[:, k * 128:(k + 1) * 128],
                                            PREFB(sl_p)[:, k * 512:(k + 1) * 512],
                                            start=(k == 0), stop=(k == NCH - 1))
                        last = P.op("tensor", mm, deps=gdeps if k == 0 else (), inc=(k == NCH - 1))
                    self.wsub_busy[sub] = last
                    gates.append(last)
                    if do + 4 < NCH:
                        wtok[do + 4] = self.load_wcf(self.wout, (do + 4) * 128, (do + 4) % 8)
                    cons = res_writer(do, tbg, bk, [last])
                    self.rel([bk], cons)
                prefb_gate[sl_p] = tmax(*gates)
        all_done = [tmax(*self.cur_x_writes.values())]
        self.finish_stage(xsrc, xdst, b3_reads, ["b_q", "b_k", "b_v"], all_done)


def build_program(n_stages=4):
    kb = KB(n_stages)
    with kb.es:
        kb.emit_all()
        block = kb.es.enter_context(kb.nc.Block())
        kb.P.build(block)
    return kb.nc



def _host_consts():
    import ml_dtypes
    i = np.arange(128, dtype=np.float32)
    c = {}
    c["c_eye_b"] = np.eye(128, dtype=np.float32)
    c["c_eye_f"] = np.eye(128, dtype=np.float32)
    dif = i[:, None] - i[None, :]
    c["c_dif"] = np.where(dif <= 0, dif, -1.0).astype(np.float32)
    c["c_mask"] = (i[:, None] <= i[None, :]).astype(np.float32)
    c["c_rowidx"] = np.ascontiguousarray(np.broadcast_to(i[None, :], (128, 128)))
    c["c_col128m"] = np.ascontiguousarray((128.0 - i)[:, None])
    c["c_onescol"] = np.ones((128, 1), dtype=np.float32)
    c["c_allones"] = np.ones((128, 128), dtype=np.float32)
    for k in ("c_eye_b", "c_onescol", "c_allones"):
        c[k] = c[k].astype(ml_dtypes.bfloat16)
    return c


def _host_prep(inputs):
    x = np.asarray(inputs["L0_attn_a_x"], dtype=np.float32)
    common = {
        "wq": np.asarray(inputs["L0_attn_a_Wq"], dtype=np.float32),
        "wk": np.asarray(inputs["L0_attn_a_Wk"], dtype=np.float32),
        "wv": np.asarray(inputs["L0_attn_a_Wv"], dtype=np.float32),
        "wo": np.asarray(inputs["L0_attn_a_Wo"], dtype=np.float32),
        "wqkv": np.asarray(inputs["L0_attn_b_Wqkv"], dtype=np.float32),
        "wout": np.asarray(inputs["L0_attn_b_Wout"], dtype=np.float32),
        "wft": np.ascontiguousarray(np.asarray(inputs["L0_ffn_weight"], dtype=np.float32).T),
        "slopes": np.asarray(inputs["L0_attn_b_slopes"], dtype=np.float32).reshape(1, H),
        "row_bv": np.asarray(inputs["L0_attn_a_bv"], dtype=np.float32).reshape(1, D),
        "row_bqv": np.ascontiguousarray(
            np.asarray(inputs["L0_attn_b_bqkv"], dtype=np.float32)[2 * D:].reshape(1, D)),
    }
    rwt = np.asarray(inputs["L0_ffn_router_w"], dtype=np.float32).T
    common["rwt"] = np.ascontiguousarray(rwt.reshape(NCH, 128, 8).transpose(1, 0, 2).reshape(128, 128))
    at = np.asarray(inputs["L0_ffn_lora_A"], dtype=np.float32)
    atf = at.transpose(2, 0, 1).reshape(D, 128)
    common["atw"] = np.ascontiguousarray(atf.reshape(NCH, 128, 128).transpose(1, 0, 2).reshape(128, NCH * 128))
    bt = np.asarray(inputs["L0_ffn_lora_B"], dtype=np.float32)
    common["btw"] = np.ascontiguousarray(bt.transpose(0, 2, 1).reshape(128, D))
    for name, arr in (("bq", inputs["L0_attn_a_bq"]), ("bk", inputs["L0_attn_a_bk"]),
                      ("bo", inputs["L0_attn_a_bo"]), ("bout", inputs["L0_attn_b_bout"])):
        common[f"pb_{name}"] = np.ascontiguousarray(
            np.asarray(arr, dtype=np.float32).reshape(NCH, 128).T)
    bqkv = np.asarray(inputs["L0_attn_b_bqkv"], dtype=np.float32)
    common["pb_bqkv_q"] = np.ascontiguousarray(bqkv[0:D].reshape(NCH, 128).T)
    common["pb_bqkv_k"] = np.ascontiguousarray(bqkv[D:2 * D].reshape(NCH, 128).T)
    common.update(_host_consts())
    in_maps = []
    for core in range(NCORE):
        xm = np.ascontiguousarray(x[core * NB:(core + 1) * NB].transpose(0, 2, 1))
        m = dict(common)
        m["xin"] = xm
        in_maps.append(m)
    return in_maps


_BUILD_CACHE = {}


def kernel(**inputs):
    n_stages = 4
    if n_stages not in _BUILD_CACHE:
        _BUILD_CACHE[n_stages] = build_program(n_stages)
    nc = _BUILD_CACHE[n_stages]
    in_maps = _host_prep(inputs)
    res = run_bass_kernel_spmd(nc, in_maps, core_ids=list(range(NCORE)))
    outs = []
    for core in range(NCORE):
        y = np.asarray(res.results[core]["y"])
        outs.append(y.transpose(0, 2, 1))
    return np.ascontiguousarray(np.concatenate(outs, axis=0))


# version 04
# speedup vs baseline: 1.1415x; 1.0024x over previous
import contextlib
import numpy as np
import concourse.bass as bass
import concourse.mybir as mybir
from concourse import bass_utils
from concourse.bass_utils import run_bass_kernel_spmd

import os as _os
if _os.environ.get("K_LDWOPT", "0") == "1" and not getattr(bass_utils, "_ldw_opt_patched", False):
    bass_utils._ldw_opt_patched = True
    _orig_bvo = bass_utils.bir_verify_and_optimise

    def _bvo_ldwopt(tmpdir, inp="bir.json", outp="file.neff", arch=None, *, dve_root=None):
        real_run = bass_utils.run_command

        def run2(cmd, **kw):
            cmd = [c.replace("--enable-ldw-opt=false", "--enable-ldw-opt=true")
                   if isinstance(c, str) else c for c in cmd]
            return real_run(cmd, **kw)
        bass_utils.run_command = run2
        try:
            return _orig_bvo(tmpdir, inp, outp, arch, dve_root=dve_root)
        finally:
            bass_utils.run_command = real_run
    bass_utils.bir_verify_and_optimise = _bvo_ldwopt

F32 = mybir.dt.float32
BF16 = mybir.dt.bfloat16
AF = mybir.ActivationFunctionType
ALU = mybir.AluOpType
AX = mybir.AxisListType

D = 2048
T = 2048
NCH = 16
H = 16
NB = 2
NCORE = 8
EPS = 1e-5

ENGINES = ("tensor", "vector", "scalar", "gpsimd", "sync")

CH_XC = 0
CH_W0 = 16
CH_W2 = 52
CH_SMALL = 18
CH_EV = 19
CH_XW = 27
CH_R0, CH_R1, CH_R2, CH_R3, CH_R4, CH_R5 = 43, 44, 45, 46, 47, 48
CH_XO0, CH_XO1 = 49, 50
CH_STAT = 51
N_DMA_CH = 54


def tmax(*toks):
    best = {}
    for t in toks:
        if t is None:
            continue
        if isinstance(t, list):
            for k, v in t:
                if best.get(k, -1) < v:
                    best[k] = v
            continue
        k, v = t
        if best.get(k, -1) < v:
            best[k] = v
    return [(k, v) for k, v in best.items()]


class Prog:
    def __init__(self, nc, es, n_dma_chans):
        self.nc = nc
        self.streams = {e: [] for e in ENGINES}
        self.counts = {e: 0 for e in ENGINES}
        self.sems = {e: es.enter_context(nc.semaphore(f"sem_{e}")) for e in ENGINES}
        self.dma_sems = [es.enter_context(nc.semaphore(f"dsem{i}")) for i in range(n_dma_chans)]
        self.dma_counts = [0] * n_dma_chans
        self.warnings = []

    def _check(self, deps):
        for key, val in deps:
            if isinstance(key, tuple) and key[0] == "dma":
                if val != self.dma_counts[key[1]]:
                    self.warnings.append(f"stale dma wait chan {key[1]}: {val} != {self.dma_counts[key[1]]}")

    def op(self, eng, fn, deps=(), inc=True, selfwait=False):
        deps = tmax(*deps) if deps else ()
        if deps:
            self._check(deps)
        tok = None
        if inc:
            self.counts[eng] += 1
            tok = (eng, self.counts[eng])
        self.streams[eng].append((fn, deps, eng if inc else None, selfwait))
        return tok

    def dma(self, queue, chan, fn, deps=()):
        deps = tmax(*deps) if deps else ()
        if deps:
            self._check(deps)
        self.dma_counts[chan] += 16
        tok = (("dma", chan), self.dma_counts[chan])
        self.streams[queue].append((fn, deps, ("dma", chan), False))
        return tok

    def _sem_of(self, key):
        if isinstance(key, tuple) and key[0] == "dma":
            return self.dma_sems[key[1]]
        return self.sems[key]

    def build(self, block):
        for eng_name in ENGINES:
            stream = self.streams[eng_name]
            deco = getattr(block, eng_name)

            def make_body(eng_name, stream):
                def body(e):
                    satisfied = {}
                    for fn, deps, inc_key, selfwait in stream:
                        for key, val in deps:
                            if key == eng_name:
                                if not selfwait:
                                    continue
                            if satisfied.get(key, -1) >= val:
                                continue
                            e.wait_ge(self._sem_of(key), val)
                            satisfied[key] = val
                        instr = fn(e)
                        if inc_key is not None and instr is not None:
                            if isinstance(inc_key, tuple):
                                instr.then_inc(self._sem_of(inc_key), 16)
                            else:
                                instr.then_inc(self.sems[inc_key], 1)
                return body

            deco(make_body(eng_name, stream))


class KB:
    def __init__(self, n_stages=4):
        self.n_stages = n_stages
        nc = bass.Bass("TRN2", target_bir_lowering=False, debug=False)
        self.nc = nc
        self.es = contextlib.ExitStack()
        dt = nc.dram_tensor

        self.xin = dt("xin", [NB, D, T], F32, kind="ExternalInput").ap()
        self.wq = dt("wq", [D, D], F32, kind="ExternalInput").ap()
        self.wk = dt("wk", [D, D], F32, kind="ExternalInput").ap()
        self.wv = dt("wv", [D, D], F32, kind="ExternalInput").ap()
        self.wo = dt("wo", [D, D], F32, kind="ExternalInput").ap()
        self.wqkv = dt("wqkv", [D, 3 * D], F32, kind="ExternalInput").ap()
        self.wout = dt("wout", [D, D], F32, kind="ExternalInput").ap()
        self.wft = dt("wft", [D, D], F32, kind="ExternalInput").ap()
        self.rwt = dt("rwt", [128, 128], F32, kind="ExternalInput").ap()
        self.atw = dt("atw", [128, NCH * 128], F32, kind="ExternalInput").ap()
        self.btw = dt("btw", [128, D], F32, kind="ExternalInput").ap()
        self.slopes = dt("slopes", [1, H], F32, kind="ExternalInput").ap()
        self.pb = {}
        for name in ("bq", "bk", "bo", "bqkv_q", "bqkv_k", "bout"):
            self.pb[name] = dt(f"pb_{name}", [128, NCH], F32, kind="ExternalInput").ap()
        self.row_bv = dt("row_bv", [1, D], F32, kind="ExternalInput").ap()
        self.row_bqv = dt("row_bqv", [1, D], F32, kind="ExternalInput").ap()
        self.c_eye_b = dt("c_eye_b", [128, 128], BF16, kind="ExternalInput").ap()
        self.c_eye_f = dt("c_eye_f", [128, 128], F32, kind="ExternalInput").ap()
        self.c_dif = dt("c_dif", [128, 128], F32, kind="ExternalInput").ap()
        self.c_mask = dt("c_mask", [128, 128], F32, kind="ExternalInput").ap()
        self.c_rowidx = dt("c_rowidx", [128, 128], F32, kind="ExternalInput").ap()
        self.c_col128m = dt("c_col128m", [128, 1], F32, kind="ExternalInput").ap()
        self.c_onescol = dt("c_onescol", [128, 1], BF16, kind="ExternalInput").ap()
        self.c_allones = dt("c_allones", [128, 128], BF16, kind="ExternalInput").ap()

        self.y = dt("y", [NB, D, T], F32, kind="ExternalOutput").ap()
        self.xs = [dt(f"xs{i}", [D, T], F32, kind="Internal").ap() for i in range(2)]
        self.a_q = dt("a_q", [H, 128, T], BF16, kind="Internal").ap()
        self.a_k = dt("a_k", [H, 128, T], BF16, kind="Internal").ap()
        self.a_v = dt("a_v", [H, 128, NCH, 128], BF16, kind="Internal").ap()
        self.b_q = dt("b_q", [NCH, 128, H, 128], BF16, kind="Internal").ap()
        self.b_k = dt("b_k", [NCH, 128, H, 128], BF16, kind="Internal").ap()
        self.b_v = dt("b_v", [NCH, 128, D], BF16, kind="Internal").ap()
        self.scr_stat = dt("scr_stat", [1, 2 * T], F32, kind="Internal").ap()
        self.scr_w8 = dt("scr_w8", [8, T], F32, kind="Internal").ap()

        es = self.es
        sb = lambda name, shape, dtyp: es.enter_context(nc.sbuf_tensor(name, shape, dtyp))
        self.XB = sb("XB", [128, NCH * T], BF16)
        self.XBF = self.XB.bitcast(F32)
        self.AR = sb("AR", [128, 18432], BF16)
        self.ARF = self.AR.bitcast(F32)
        self.WCB = [sb(f"WCB{i}", [128, NCH * 512], BF16) for i in range(2)]
        self.STV = sb("STV", [1, 3 * T], F32)
        self.EYEB = sb("EYEB", [128, 128], BF16)
        self.EYEF = sb("EYEF", [128, 128], F32)
        self.DIF = sb("DIF", [128, 128], F32)
        self.MASK = sb("MASK", [128, 128], F32)
        self.ROWIDX = sb("ROWIDX", [128, 128], F32)
        self.COLM = sb("COLM", [128, 1], F32)
        self.ONESC = sb("ONESC", [128, 1], BF16)
        self.ALLONES = sb("ALLONES", [128, 128], BF16)
        self.SLP = sb("SLP", [128, 64], F32)
        self.INTRA = sb("INTRA", [128, H * 128], BF16)
        self.QDECF = sb("QDECF", [128, H * 128], BF16)
        self.RWT = sb("RWT", [128, 128], BF16)
        self.ATW = sb("ATW", [128, NCH * 128], BF16)
        self.BTW = sb("BTW", [128, D], BF16)
        self.PBS = {name: sb(f"PBS_{name}", [128, NCH], F32) for name in self.pb}
        self.WTB = sb("WTB", [8, T], F32)
        self.W8S = sb("W8S", [128, 48], F32)

        self.PD = [es.enter_context(nc.psum_tensor(f"PD{i}", [128, 1024], F32)) for i in range(4)]
        self.PDB = self.PD[3].bitcast(BF16)
        self.bank_state = [[] for _ in range(8)]

        self.P = Prog(nc, es, N_DMA_CH)

        self.xb_consumed = None
        self.xb_free = [None, None]
        self.next_x_ready = [None] * NCH
        self.buf_gate = {}
        self.wsub_busy = [None] * 8
        self.evb_busy = [None] * 8
        self.scr_gate = {}
        self.taps_on = False
        self.tap_toks = []
        self._tapn = 0
        self._tapnames = set()

    def bank(self, i):
        return self.PD[i // 2][:, (i % 2) * 512:(i % 2 + 1) * 512]

    def bank_row0(self, i):
        return self.PD[i // 2][0:1, (i % 2) * 512:(i % 2 + 1) * 512]

    def acq(self, banks, *extra):
        deps = []
        for b in banks:
            deps += self.bank_state[b]
        for e in extra:
            if e is None:
                continue
            deps += e if isinstance(e, list) else [e]
        return deps

    def rel(self, banks, toks):
        toks = toks if isinstance(toks, list) else [toks]
        for b in banks:
            self.bank_state[b] = [t for t in toks if t is not None]

    def arb(self, off, cols):
        return self.AR[:, off:off + cols]

    def arf(self, off, cols):
        return self.ARF[:, off // 2:off // 2 + cols]

    def wcf(self, i):
        return self.WCB[i // 4][:, (i % 4) * 2048:(i % 4) * 2048 + 2048]

    def load_wcf(self, wsrc, col0, i, extra=()):
        deps = list(extra)
        for j in (i, i ^ 4):
            if self.wsub_busy[j] is not None:
                deps.append(self.wsub_busy[j])
        dst = self.wcf(i)

        def fn(g, wsrc=wsrc, col0=col0, dst=dst):
            return g.dma_start(out=dst.rearrange("p (k f) -> p k f", f=128),
                               in_=wsrc[:, col0:col0 + 128].rearrange("(k p) f -> p k f", p=128))
        ch = (CH_W0 + i % 2) if i % 4 < 2 else (CH_W2 + i % 2)
        return self.P.dma("gpsimd", ch, fn, deps=deps)

    def load_wcb(self, wsrc, col0, wblk, slot, extra=()):
        deps = list(extra)
        for i in range(8):
            if self.wsub_busy[i] is not None:
                deps.append(self.wsub_busy[i])
        dst = self.WCB[slot][:, 0:NCH * wblk]

        def fn(g, wsrc=wsrc, col0=col0, wblk=wblk, dst=dst):
            return g.dma_start(out=dst.rearrange("p (k f) -> p k f", f=wblk),
                               in_=wsrc[:, col0:col0 + wblk].rearrange("(k p) f -> p k f", p=128))
        return self.P.dma("gpsimd", CH_W0 + slot, fn, deps=deps)

    def set_wcb_busy(self, slot, tok):
        for i in range(slot * 4, slot * 4 + 4):
            self.wsub_busy[i] = tok

    def evb(self, slot):
        return self.arb(slot * 512, 512)

    def tap(self, name, ap, deps, dtype=None):
        if not self.taps_on:
            return
        shape = list(ap.shape)
        dtyp = dtype or ap.dtype
        dten = self.nc.dram_tensor(f"tap_{name}_{self._tapn}" if name in self._tapnames else f"tap_{name}",
                                   shape, dtyp, kind="ExternalOutput").ap()
        self._tapnames.add(name)
        self._tapn += 1
        tok = self.P.dma("sync", CH_STAT, lambda s, dten=dten, ap=ap: s.dma_start(out=dten[:, :], in_=ap),
                         deps=deps + (self.scr_gate.get("scr_stat") or []))
        self.tap_toks.append(tok)
        self.scr_gate.setdefault("taps", []).append(tok)
        return tok

    def emit_all(self):
        self.load_consts()
        self.final_writes = []
        stages = ["a", "f", "b", "f"][: self.n_stages]
        for b in range(NB):
            xsrc = self.xin[b]
            self.next_x_ready = [None] * NCH
            for si, kind in enumerate(stages):
                last = si == len(stages) - 1
                xdst = self.y[b] if last else self.xs[si % 2]
                if kind == "a":
                    self.emit_attn_a(xsrc, xdst)
                elif kind == "f":
                    self.emit_ffn(xsrc, xdst)
                else:
                    self.emit_attn_b(xsrc, xdst)
                xsrc = xdst
        self.P.op("sync", lambda s: s.nop(), deps=[tmax(*self.final_writes, *self.tap_toks)], inc=False)

    def load_consts(self):
        P = self.P
        pairs = [
            (self.EYEB, self.c_eye_b), (self.EYEF, self.c_eye_f), (self.DIF, self.c_dif),
            (self.MASK, self.c_mask), (self.ROWIDX, self.c_rowidx), (self.COLM, self.c_col128m),
            (self.ONESC, self.c_onescol), (self.ALLONES, self.c_allones),
            (self.RWT, self.rwt), (self.ATW, self.atw), (self.BTW, self.btw),
        ]
        t_all = None
        for dst, src in pairs:
            t_all = P.dma("gpsimd", CH_SMALL, lambda g, dst=dst, src=src: g.dma_start(out=dst[:], in_=src[:, :]))
        for name in self.pb:
            t_all = P.dma("gpsimd", CH_SMALL,
                          lambda g, d=self.PBS[name], s=self.pb[name]: g.dma_start(out=d[:], in_=s[:, :]))
        t_all = P.dma("gpsimd", CH_SMALL,
                      lambda g: g.dma_start(out=self.SLP[:, 0:16], in_=self.slopes.broadcast_to([128, H])))
        t_neg = P.op("vector", lambda v: v.tensor_scalar_mul(self.SLP[:, 16:32], self.SLP[:, 0:16], -1.0),
                     deps=[t_all])
        t_bd = P.op("scalar", lambda s: s.activation(self.SLP[:, 32:48], self.SLP[:, 0:16], AF.Exp, scale=-128.0),
                    deps=[t_all])
        out_toks = [t_neg, t_bd]
        for h in range(H):
            out_toks.append(P.op("scalar", lambda s, h=h: s.activation(
                self.SLP[:, 48 + h:49 + h], self.COLM[:, 0:1], AF.Exp,
                scale=self.SLP[:, 16 + h:17 + h]), deps=[t_neg, t_all]))
        tmp = self.arf(0, 128)
        prev = None
        for h in range(H):
            t1 = P.op("scalar", lambda s, h=h: s.activation(
                tmp, self.DIF[:, :], AF.Exp, scale=self.SLP[:, 0 + h:1 + h]),
                deps=[t_all] + ([prev] if prev else []))
            t2 = P.op("vector", lambda v, h=h: v.tensor_mul(
                self.INTRA[:, h * 128:(h + 1) * 128], tmp, self.MASK[:, :]),
                deps=[t1, t_all])
            prev = t2
            out_toks.append(t2)
        for h in range(H):
            out_toks.append(P.op("scalar", lambda s, h=h: s.activation(
                self.QDECF[:, h * 128:(h + 1) * 128], self.ROWIDX[:, :], AF.Exp,
                scale=self.SLP[:, 16 + h:17 + h]), deps=[t_neg, t_all]))
        self.const_toks = tmax(t_all, *out_toks)
        self.xb_consumed = tmax(*out_toks)

    def emit_stats_cast(self, xsrc):
        P = self.P
        xc = []
        for c in range(NCH):
            deps = [self.xb_consumed, self.next_x_ready[c]]
            xc.append(P.dma("gpsimd", CH_XC + c, lambda g, c=c, xsrc=xsrc: g.dma_start(
                out=self.XB[:, c * T:(c + 1) * T], in_=xsrc[c * 128:(c + 1) * 128, :]), deps=deps))
        self.xsrc_reads = list(xc)
        deps0 = self.acq([4, 5, 6, 7], [self.const_toks])
        last1 = {}
        for c in range(NCH):
            for tb in range(4):
                def mm(t, c=c, tb=tb):
                    return t.matmul(self.bank_row0(4 + tb), self.ONESC[:, 0:1],
                                    self.XB[:, c * T + tb * 512: c * T + tb * 512 + 512],
                                    start=(c == 0), stop=(c == NCH - 1))
                last1[tb] = P.op("tensor", mm, deps=deps0 + [xc[c]], inc=(c == NCH - 1))
        mtok = []
        for tb in range(4):
            ev = P.op("vector", lambda v, tb=tb: v.tensor_scalar_mul(
                self.STV[0:1, tb * 512:(tb + 1) * 512], self.bank_row0(4 + tb), 1.0 / D), deps=[last1[tb]])
            mtok.append(ev)
            self.rel([4 + tb], [ev])
        deps2 = self.acq([4, 5, 6, 7])
        sq_free = [None, None]
        last2 = {}
        for c in range(NCH):
            slot = c % 2
            sq = self.arb(slot * 2048, 2048)
            sdeps = [xc[c], self.xb_consumed]
            if sq_free[slot] is not None:
                sdeps.append(sq_free[slot])
            tsq = P.op("scalar", lambda s, c=c, sq=sq: s.activation(
                sq, self.XB[:, c * T:(c + 1) * T], AF.Square), deps=sdeps)
            for tb in range(4):
                def mm(t, sq=sq, tb=tb, c=c):
                    return t.matmul(self.bank_row0(4 + tb), self.ONESC[:, 0:1],
                                    sq[:, tb * 512: tb * 512 + 512],
                                    start=(c == 0), stop=(c == NCH - 1))
                last2[tb] = P.op("tensor", mm, deps=deps2 + [tsq], inc=(c == NCH - 1 or tb == 3))
            sq_free[slot] = last2[3]
        qtok = []
        for tb in range(4):
            ev = P.op("vector", lambda v, tb=tb: v.tensor_scalar_mul(
                self.STV[0:1, T + tb * 512: T + (tb + 1) * 512], self.bank_row0(4 + tb), 1.0 / D),
                deps=[last2[tb]])
            qtok.append(ev)
            self.rel([4 + tb], [ev])
        self.sq_mm_done = qtok
        ftok, ntok = [], []
        for tb in range(4):
            m = self.STV[0:1, tb * 512:(tb + 1) * 512]
            q = self.STV[0:1, T + tb * 512: T + (tb + 1) * 512]
            iv = self.STV[0:1, 2 * T + tb * 512: 2 * T + (tb + 1) * 512]
            t1 = P.op("vector", lambda v, m=m, iv=iv: v.tensor_mul(iv, m, m), deps=[mtok[tb], qtok[tb]])
            t2 = P.op("vector", lambda v, q=q, iv=iv: v.tensor_sub(q, q, iv), deps=[t1])
            t3 = P.op("vector", lambda v, q=q: v.tensor_scalar_add(q, q, EPS), deps=[t2])
            t4 = P.op("vector", lambda v, q=q, iv=iv: v.reciprocal(iv, q), deps=[t3])
            t5 = P.op("scalar", lambda s, iv=iv: s.activation(iv, iv, AF.Sqrt), deps=[t4])
            tn = P.op("vector", lambda v, m=m: v.tensor_scalar_mul(m, m, -1.0), deps=[t1])
            ftok.append(t5)
            ntok.append(tn)
        g = self.scr_gate.get("scr_stat") or []
        tw = P.dma("sync", CH_STAT, lambda s: s.dma_start(out=self.scr_stat[0:1, 0:T],
                                                          in_=self.STV[0:1, 0:T]), deps=ntok + g)
        NMB = self.arf(4096, 2048)
        INVB = self.arf(8192, 2048)
        tb1 = P.dma("sync", CH_R0, lambda s: s.dma_start(
            out=NMB, in_=self.scr_stat[0:1, 0:T].broadcast_to([128, T])), deps=[tw, self.xb_consumed])
        tw2 = P.dma("sync", CH_STAT, lambda s: s.dma_start(out=self.scr_stat[0:1, T:2 * T],
                                                           in_=self.STV[0:1, 2 * T:3 * T]), deps=ftok)
        tb2 = P.dma("sync", CH_R1, lambda s: s.dma_start(
            out=INVB, in_=self.scr_stat[0:1, T:2 * T].broadcast_to([128, T])), deps=[tw2, self.xb_consumed])
        self.scr_gate["scr_stat"] = [tb1, tb2]
        xnb = []
        prevv = None
        for c in range(NCH):
            eng = "gpsimd" if c % 3 == 2 else "vector"
            off = 12288 if eng == "vector" else 14336
            TM = self.arf(off, 1024)
            d1 = [xc[c], tb1, self.xb_consumed]
            if eng == "vector" and prevv is not None:
                d1.append(prevv)
            t1a = P.op(eng, lambda v, c=c, TM=TM: v.tensor_add(TM, self.XB[:, c * T:c * T + 1024], NMB[:, 0:1024]),
                       deps=d1)
            t1b = P.op(eng, lambda v, c=c, TM=TM: v.tensor_mul(self.XB[:, c * T:c * T + 1024], TM, INVB[:, 0:1024]),
                       deps=[t1a, tb2])
            t2a = P.op(eng, lambda v, c=c, TM=TM: v.tensor_add(TM, self.XB[:, c * T + 1024:(c + 1) * T],
                                                               NMB[:, 1024:2048]), deps=[t1b])
            t2b = P.op(eng, lambda v, c=c, TM=TM: v.tensor_mul(self.XB[:, c * T + 1024:(c + 1) * T], TM,
                                                               INVB[:, 1024:2048]), deps=[t2a])
            if eng == "vector":
                prevv = t2b
            xnb.append(tmax(t1b, t2b))
        self.xnb_tok = xnb
        self.xnb_all = tmax(*xnb)
        self.stat_busy = [tb1, tb2]
        return xnb

    def proj_feature(self, wsrc, wcol0, writer, n_out=NCH):
        P = self.P
        wtok = {}
        for d0 in range(min(4, n_out)):
            wtok[d0] = self.load_wcf(wsrc, wcol0 + d0 * 128, d0 % 8)
        for do in range(n_out):
            sub = do % 8
            banks = [(do % 2) * 4 + tb for tb in range(4)]
            gdeps = self.acq(banks, [wtok[do]])
            last = None
            for k in range(NCH):
                for tb in range(4):
                    def mm(t, k=k, tb=tb, bk=banks[tb], sub=sub):
                        return t.matmul(self.bank(bk),
                                        self.wcf(sub)[:, k * 128:(k + 1) * 128],
                                        self.XB[:, k * T + tb * 512: k * T + tb * 512 + 512],
                                        start=(k == 0), stop=(k == NCH - 1))
                    last = P.op("tensor", mm, deps=(gdeps + [self.xnb_tok[k]]) if tb == 0 else (),
                                inc=(k == NCH - 1 and tb == 3))
            self.wsub_busy[sub] = last
            if do + 4 < n_out:
                wtok[do + 4] = self.load_wcf(wsrc, wcol0 + (do + 4) * 128, (do + 4) % 8)
            for tb in range(4):
                cons = writer(do, tb, banks[tb], [last])
                self.rel([banks[tb]], cons)

    def proj_token(self, wsrc, wcol0, wcols, writer):
        P = self.P
        n_dob = wcols // 512
        wtok = {0: self.load_wcb(wsrc, wcol0, 512, 0)}
        if n_dob > 1:
            wtok[1] = self.load_wcb(wsrc, wcol0 + 512, 512, 1)
        for dob in range(n_dob):
            slot = dob % 2
            last = None
            for tc in range(NCH):
                bk = (tc % 2) * 4 + dob
                gdeps = self.acq([bk], [wtok[dob]])
                for k in range(NCH):
                    def mm(t, k=k, tc=tc, bk=bk, slot=slot):
                        return t.matmul(self.bank(bk),
                                        self.XB[:, k * T + tc * 128: k * T + tc * 128 + 128],
                                        self.WCB[slot][:, k * 512:(k + 1) * 512],
                                        start=(k == 0), stop=(k == NCH - 1))
                    last = P.op("tensor", mm,
                                deps=(gdeps + [self.xnb_tok[k]]) if k == 0 else [self.xnb_tok[k]],
                                inc=(k == NCH - 1))
                cons = writer(tc, dob, bk, [last])
                self.rel([bk], cons)
            self.set_wcb_busy(slot, last)
            if dob + 2 < n_dob:
                wtok[dob + 2] = self.load_wcb(wsrc, wcol0 + (dob + 2) * 512, 512, slot)

    def feat_writer(self, dst_fn, bias_name, scr_key):
        P = self.P

        def writer(do, tb, bk, deps):
            slot = (do * 4 + tb) % 4
            ev = self.evb(slot)
            d = list(deps)
            if self.evb_busy[slot] is not None:
                d.append(self.evb_busy[slot])
            bias = self.PBS[bias_name][:, do:do + 1]
            tok = P.op("vector", lambda v, ev=ev, bk=bk, bias=bias: v.tensor_scalar(
                ev, self.bank(bk), bias, None, ALU.add), deps=d)
            wd = [tok] + (self.scr_gate.get(scr_key) or [])

            def dfn(s, do=do, tb=tb, ev=ev):
                o, i = dst_fn(do, tb, ev)
                return s.dma_start(out=o, in_=i)
            wtok = P.dma("sync", CH_EV + slot, dfn, deps=wd)
            self.evb_busy[slot] = wtok
            self.cur_scr_writes.setdefault(scr_key, []).append(wtok)
            return [tok]
        return writer

    def tok_writer(self, dst_fn, bvb_ap, bvb_tok, scr_key):
        P = self.P

        def writer(tc, dob, bk, deps):
            slot = (tc * 4 + dob) % 4 + 4
            ev = self.evb(slot)
            d = list(deps) + [bvb_tok]
            if self.evb_busy[slot] is not None:
                d.append(self.evb_busy[slot])
            tok = P.op("vector", lambda v, ev=ev, bk=bk, dob=dob: v.tensor_add(
                ev, self.bank(bk), bvb_ap[:, dob * 512:(dob + 1) * 512]), deps=d)
            wd = [tok] + (self.scr_gate.get(scr_key) or [])

            def dfn(s, tc=tc, dob=dob, ev=ev):
                o, i = dst_fn(tc, dob, ev)
                return s.dma_start(out=o, in_=i)
            wtok = P.dma("sync", CH_EV + slot, dfn, deps=wd)
            self.evb_busy[slot] = wtok
            self.cur_scr_writes.setdefault(scr_key, []).append(wtok)
            return [tok]
        return writer

    def residual_writer(self, xsrc, xdst, bias_name):
        P = self.P
        state = {"xo": [None, None], "ev": [None, None]}

        def writer(do, tbg, bk, deps):
            xslot = (do * 4 + tbg) % 2
            XOLD = self.arf(12288 + xslot * 1024, 512)
            EVF = self.arf(14336 + xslot * 1024, 512)
            xod = [state["xo"][xslot], self.next_x_ready[do], self.xnb_all]
            txo = P.dma("sync", CH_XO0 + xslot, lambda s, do=do, tbg=tbg, XOLD=XOLD: s.dma_start(
                out=XOLD, in_=xsrc[do * 128:(do + 1) * 128, tbg * 512:(tbg + 1) * 512]), deps=xod)
            self.xsrc_reads.append(txo)
            evd = [state["ev"][xslot], txo, self.xnb_all] + list(deps)
            if bias_name is not None:
                bias = self.PBS[bias_name][:, do:do + 1]
                tev = P.op("vector", lambda v, bk=bk, XOLD=XOLD, EVF=EVF, bias=bias: v.scalar_tensor_tensor(
                    EVF, self.bank(bk), bias, XOLD, ALU.add, ALU.add), deps=evd)
            else:
                tev = P.op("vector", lambda v, bk=bk, XOLD=XOLD, EVF=EVF: v.tensor_add(
                    EVF, self.bank(bk), XOLD), deps=evd)
            wd = [tev] + (self.buf_gate.get(id(xdst)) or [])
            twr = P.dma("sync", CH_XW + do, lambda s, do=do, tbg=tbg, EVF=EVF: s.dma_start(
                out=xdst[do * 128:(do + 1) * 128, tbg * 512:(tbg + 1) * 512], in_=EVF), deps=wd)
            state["xo"][xslot] = tev
            state["ev"][xslot] = twr
            self.cur_x_writes[(do, tbg)] = twr
            return [tev]
        return writer

    def begin_stage(self):
        self.cur_scr_writes = {}
        self.cur_x_writes = {}
        self.xsrc_reads = []

    def finish_stage(self, xsrc, xdst, extra_reads, scr_keys_read, all_done_toks):
        for k in scr_keys_read:
            self.scr_gate[k] = list(tmax(*extra_reads))
        self.buf_gate[id(xsrc)] = list(tmax(*self.xsrc_reads))
        self.next_x_ready = [tmax(*(self.cur_x_writes.get((c, tb)) for tb in range(4))) for c in range(NCH)]
        self.xb_consumed = tmax(*all_done_toks)
        self.final_writes += list(self.cur_x_writes.values())
        self.buf_gate.pop(id(xdst), None)

    def emit_attn_a(self, xsrc, xdst):
        P = self.P
        self.begin_stage()
        self.emit_stats_cast(xsrc)
        self.proj_feature(self.wk, 0, self.feat_writer(
            lambda do, tb, ev: (self.a_k[do, :, tb * 512:(tb + 1) * 512], ev), "bk", "a_k"))
        self.proj_feature(self.wq, 0, self.feat_writer(
            lambda do, tb, ev: (self.a_q[do, :, tb * 512:(tb + 1) * 512], ev), "bq", "a_q"))
        BVB = self.arf(4096, 2048)
        bv_tok = P.dma("sync", CH_R2, lambda s: s.dma_start(
            out=BVB, in_=self.row_bv.broadcast_to([128, D])), deps=self.xnb_tok + [self.stat_busy[0]])

        def av_dst(tc, dob, ev):
            return (self.a_v[dob * 4:(dob + 1) * 4, :, tc, :].rearrange("h p f -> p h f"),
                    ev.rearrange("p (h f) -> p h f", f=128))
        self.proj_token(self.wv, 0, D, self.tok_writer(av_dst, BVB, bv_tok, "a_v"))
        aq_all = tmax(*self.cur_scr_writes["a_q"])
        ak_all = tmax(*self.cur_scr_writes["a_k"])
        av_all = tmax(*self.cur_scr_writes["a_v"])

        KT = lambda s: self.arb(0 + s * 2048, 2048)
        VH = lambda s: self.arb(4096 + s * 2048, 2048)
        QT = lambda s: self.arb(8192 + s * 1024, 1024)
        RB = self.arf(10240, 1024)
        expST = lambda kt: self.XB[:, kt * 1024:(kt + 1) * 1024]
        PREF = lambda h: self.XB[:, 16384 + h * 1024: 16384 + (h + 1) * 1024]
        phase_gate = tmax(aq_all, ak_all, av_all, self.xnb_all, *self.stat_busy, bv_tok)

        kt_busy = [None, None]
        vh_busy = [None, None]
        qt_busy = [None, None]
        exp_free = {}
        pref_gate = {}
        a3_reads = []
        res_writer = self.residual_writer(xsrc, xdst, "bo")
        for qbp in range(2):
            for h in range(H):
                slot = h % 2
                tk = P.dma("sync", CH_R0 + slot, lambda s, h=h, slot=slot: s.dma_start(
                    out=KT(slot)[:, :], in_=self.a_k[h, :, :]),
                    deps=[kt_busy[slot], phase_gate])
                tv = P.dma("sync", CH_R2 + slot, lambda s, h=h, slot=slot: s.dma_start(
                    out=VH(slot)[:, :].rearrange("p (kt f) -> p kt f", f=128),
                    in_=self.a_v[h]), deps=[vh_busy[slot], phase_gate])
                tq = P.dma("sync", CH_R4 + slot, lambda s, h=h, qbp=qbp, slot=slot: s.dma_start(
                    out=QT(slot)[:, :], in_=self.a_q[h, :, qbp * 1024:(qbp + 1) * 1024]),
                    deps=[qt_busy[slot], phase_gate])
                a3_reads += [tk, tv, tq]
                exp_toks = {}
                last_st = None
                for kt in range(NCH):
                    pdi = kt % 2
                    deps = self.acq([pdi * 2, pdi * 2 + 1], [tk, tq])
                    P.op("tensor", lambda t, kt=kt, pdi=pdi, slot=slot: t.matmul(
                        self.PD[pdi][:, 0:512], KT(slot)[:, kt * 128:(kt + 1) * 128],
                        QT(slot)[:, 0:512], start=True, stop=True), deps=deps, inc=False)
                    m1 = P.op("tensor", lambda t, kt=kt, pdi=pdi, slot=slot: t.matmul(
                        self.PD[pdi][:, 512:1024], KT(slot)[:, kt * 128:(kt + 1) * 128],
                        QT(slot)[:, 512:1024], start=True, stop=True))
                    te = P.op("scalar", lambda s, kt=kt, pdi=pdi: s.activation(
                        expST(kt), self.PD[pdi][:, :], AF.Exp, scale=1.0 / 128.0),
                        deps=[m1, exp_free.get(kt), phase_gate if (qbp == 0 and h == 0) else None])
                    exp_toks[kt] = te
                    last_st = m1
                    self.rel([pdi * 2, pdi * 2 + 1], [te])
                av0 = self.acq([4], [tv])
                av1 = self.acq([5])
                sm = self.acq([6, 7])
                last_av = None
                for kt in range(NCH):
                    P.op("tensor", lambda t, kt=kt, slot=slot: t.matmul(
                        self.PD[2][:, 0:512], VH(slot)[:, kt * 128:(kt + 1) * 128],
                        expST(kt)[:, 0:512], start=(kt == 0), stop=(kt == NCH - 1)),
                        deps=(av0 if kt == 0 else []) + [exp_toks[kt]], inc=False)
                    P.op("tensor", lambda t, kt=kt, slot=slot: t.matmul(
                        self.PD[2][:, 512:1024], VH(slot)[:, kt * 128:(kt + 1) * 128],
                        expST(kt)[:, 512:1024], start=(kt == 0), stop=(kt == NCH - 1)),
                        deps=av1 if kt == 0 else (), inc=False)
                    P.op("tensor", lambda t, kt=kt: t.matmul(
                        self.PD[3][:, 0:512], self.ALLONES[:, :], expST(kt)[:, 0:512],
                        start=(kt == 0), stop=(kt == NCH - 1)), deps=sm if kt == 0 else (), inc=False)
                    s1 = P.op("tensor", lambda t, kt=kt: t.matmul(
                        self.PD[3][:, 512:1024], self.ALLONES[:, :], expST(kt)[:, 512:1024],
                        start=(kt == 0), stop=(kt == NCH - 1)))
                    exp_free[kt] = s1
                    last_av = s1
                r0 = P.op("vector", lambda v: v.reciprocal(RB[:, 0:512], self.PD[3][:, 0:512]),
                          deps=[last_av, phase_gate if (qbp == 0 and h == 0) else None])
                r1 = P.op("vector", lambda v: v.reciprocal(RB[:, 512:1024], self.PD[3][:, 512:1024]),
                          deps=[r0])
                self.rel([6, 7], [r1])
                e0 = P.op("vector", lambda v, h=h: v.tensor_mul(
                    PREF(h)[:, 0:512], self.PD[2][:, 0:512], RB[:, 0:512]),
                    deps=[last_av, r0] + (pref_gate.get(h) or []))
                e1 = P.op("vector", lambda v, h=h: v.tensor_mul(
                    PREF(h)[:, 512:1024], self.PD[2][:, 512:1024], RB[:, 512:1024]), deps=[r1])
                self.rel([4], [e0])
                self.rel([5], [e1])
                kt_busy[slot] = last_st
                qt_busy[slot] = last_st
                vh_busy[slot] = last_av
                pref_gate[h] = [e0, e1]
            wtok = {d0: self.load_wcf(self.wo, d0 * 128, d0 % 8) for d0 in range(4)}
            new_pref_gate = {}
            for do in range(NCH):
                sub = do % 8
                banks = [(do % 2) * 4 + qb for qb in range(2)]
                gdeps = self.acq(banks, [wtok[do]])
                for h in range(H):
                    gdeps += pref_gate.get(h) or []
                last = None
                for k in range(NCH):
                    for qb in range(2):
                        def mm(t, k=k, qb=qb, bk=banks[qb], sub=sub):
                            return t.matmul(self.bank(bk),
                                            self.wcf(sub)[:, k * 128:(k + 1) * 128],
                                            PREF(k)[:, qb * 512:(qb + 1) * 512],
                                            start=(k == 0), stop=(k == NCH - 1))
                        last = P.op("tensor", mm, deps=gdeps if (k == 0 and qb == 0) else (),
                                    inc=(k == NCH - 1 and qb == 1))
                self.wsub_busy[sub] = last
                for h in range(H):
                    new_pref_gate.setdefault(h, []).append(last)
                if do + 4 < NCH:
                    wtok[do + 4] = self.load_wcf(self.wo, (do + 4) * 128, (do + 4) % 8)
                for qb in range(2):
                    cons = res_writer(do, qbp * 2 + qb, banks[qb], [last])
                    self.rel([banks[qb]], cons)
            pref_gate = new_pref_gate
        all_done = [tmax(*(pref_gate[h] for h in range(H)), *self.cur_x_writes.values())]
        self.finish_stage(xsrc, xdst, a3_reads, ["a_q", "a_k", "a_v"], all_done)

    def emit_ffn(self, xsrc, xdst):
        P = self.P
        self.begin_stage()
        self.emit_stats_cast(xsrc)
        lg_evac = []
        for tb in range(4):
            bk = 4 + tb % 2
            gdeps = self.acq([bk])
            last = None
            for k in range(NCH):
                def mm(t, k=k, tb=tb, bk=bk):
                    return t.matmul(self.PD[bk // 2][0:8, (bk % 2) * 512:(bk % 2) * 512 + 512],
                                    self.RWT[:, k * 8:(k + 1) * 8],
                                    self.XB[:, k * T + tb * 512: k * T + tb * 512 + 512],
                                    start=(k == 0), stop=(k == NCH - 1))
                last = P.op("tensor", mm, deps=gdeps + [self.xnb_tok[k]], inc=(k == NCH - 1))
            ev = P.op("vector", lambda v, tb=tb, bk=bk: v.tensor_copy(
                self.arf(0, 2048)[0:8, tb * 512:(tb + 1) * 512],
                self.PD[bk // 2][0:8, (bk % 2) * 512:(bk % 2) * 512 + 512]),
                deps=[last] + self.sq_mm_done)
            self.rel([bk], [ev])
            lg_evac.append(ev)
        LGTF = self.arf(0, 2048)
        self.tap("lgt", LGTF[0:8, :], lg_evac)
        AXW = self.arb(8192, 2048)
        axt_last = {}
        for tb in range(4):
            bk = 4 + tb % 2
            gdeps = self.acq([bk])
            last = None
            for k in range(NCH):
                def mm(t, k=k, tb=tb, bk=bk):
                    return t.matmul(self.bank(bk), self.ATW[:, k * 128:(k + 1) * 128],
                                    self.XB[:, k * T + tb * 512: k * T + tb * 512 + 512],
                                    start=(k == 0), stop=(k == NCH - 1))
                last = P.op("tensor", mm, deps=gdeps + [self.xnb_tok[k]], inc=(k == NCH - 1))
            axt_last[tb] = last
            tx = P.op("vector", lambda v, tb=tb, bk=bk: v.tensor_scalar_mul(
                AXW[:, tb * 512:(tb + 1) * 512], self.bank(bk), 2.0),
                deps=[last, self.xnb_all, self.stat_busy[1]])
            self.rel([bk], [tx])
            axt_last[tb] = tx
        w8 = self.W8S
        w8_done = []
        for ti in range(NCH):
            tb = ti // 4
            d0 = self.acq([6], [lg_evac[tb]])
            t_tr = P.op("tensor", lambda t, ti=ti: t.transpose(
                self.PD[3][:, 0:8], LGTF[0:8, ti * 128:(ti + 1) * 128], self.EYEF[0:8, 0:8]), deps=d0)
            Lp = self.PD[3][:, 0:8]
            Lc = w8[:, 0:8]
            m1, m2, rc = w8[:, 40:41], w8[:, 41:42], w8[:, 42:43]
            E, Msk, Ew, W8 = w8[:, 8:16], w8[:, 16:24], w8[:, 24:32], w8[:, 32:40]
            v0 = P.op("vector", lambda v: v.tensor_copy(Lc, Lp),
                      deps=[t_tr, self.xnb_all] if ti == 0 else [t_tr])
            self.rel([6], [v0])
            s1 = P.op("scalar", lambda s: s.activation(E, Lc, AF.Exp), deps=[v0])
            v1 = P.op("vector", lambda v: v.reduce_max(m1, Lc, axis=AX.X), deps=[v0], selfwait=True)
            v3 = P.op("vector", lambda v: v.tensor_scalar(Msk, Lc, m1, -1e30, ALU.is_ge, ALU.mult),
                      deps=[v1], selfwait=True)
            v4 = P.op("vector", lambda v: v.tensor_tensor(Ew, Lc, Msk, ALU.add), deps=[v3], selfwait=True)
            v5 = P.op("vector", lambda v: v.reduce_max(m2, Ew, axis=AX.X), deps=[v4], selfwait=True)
            v6 = P.op("vector", lambda v: v.tensor_scalar(Msk, Lc, m2, 1.0, ALU.is_ge, ALU.mult),
                      deps=[v5, s1], selfwait=True)
            v7 = P.op("vector", lambda v: v.tensor_mul(Ew, E, Msk), deps=[v6], selfwait=True)
            v8 = P.op("vector", lambda v: v.tensor_reduce(rc, Ew, axis=AX.X, op=ALU.add),
                      deps=[v7], selfwait=True)
            v9 = P.op("vector", lambda v: v.reciprocal(rc, rc), deps=[v8], selfwait=True)
            va = P.op("vector", lambda v: v.tensor_scalar(W8, Ew, rc, None, ALU.mult),
                      deps=[v9], selfwait=True)
            d1 = self.acq([7])
            t_tw = P.op("tensor", lambda t: t.transpose(self.PD[3][0:8, 512:640], W8, self.EYEF[:, :]),
                        deps=d1 + [va])
            vb = P.op("vector", lambda v, ti=ti: v.tensor_copy(
                self.WTB[0:8, ti * 128:(ti + 1) * 128], self.PD[3][0:8, 512:640]), deps=[t_tw])
            self.rel([7], [vb])
            w8_done.append(vb)
            if ti == 0 and self.taps_on:
                ttok = self.tap("w8all", w8[:, 0:48], [va, s1])
                self.bank_state[6] = self.bank_state[6] + [ttok]
        WB = self.arf(4096, 2048)
        gw = self.scr_gate.get("scr_w8") or []
        t_w = P.dma("sync", CH_STAT, lambda s: s.dma_start(out=self.scr_w8[:, :], in_=self.WTB[:, :]),
                    deps=w8_done + gw)
        wb_toks = []
        for e in range(8):
            wb_toks.append(P.dma("sync", CH_R3, lambda s, e=e: s.dma_start(
                out=WB[e * 16:(e + 1) * 16, :],
                in_=self.scr_w8[e:e + 1, :].broadcast_to([16, T])),
                deps=[t_w, self.xnb_all, self.stat_busy[0]]))
        wb_all = tmax(*wb_toks)
        self.scr_gate["scr_w8"] = wb_toks
        self.tap("wtb", self.WTB[:, :], w8_done)
        self.tap("wb", WB[:, :], wb_toks)
        axw_toks = []
        for tb in range(4):
            tx = P.op("vector", lambda v, tb=tb: v.tensor_mul(
                AXW[:, tb * 512:(tb + 1) * 512], AXW[:, tb * 512:(tb + 1) * 512],
                WB[:, tb * 512:(tb + 1) * 512]),
                deps=[axt_last[tb], wb_all])
            axw_toks.append(tx)
        self.tap("axw", AXW[:, :], axw_toks)
        res_writer = self.residual_writer(xsrc, xdst, None)
        wtok = {d0: self.load_wcf(self.wft, d0 * 128, d0 % 8) for d0 in range(4)}
        for do in range(NCH):
            sub = do % 8
            banks = [(do % 2) * 4 + tb for tb in range(4)]
            gdeps = self.acq(banks, [wtok[do]])
            for k in range(NCH):
                for tb in range(4):
                    def mm(t, k=k, tb=tb, bk=banks[tb], sub=sub):
                        return t.matmul(self.bank(bk),
                                        self.wcf(sub)[:, k * 128:(k + 1) * 128],
                                        self.XB[:, k * T + tb * 512: k * T + tb * 512 + 512],
                                        start=(k == 0), stop=False)
                    P.op("tensor", mm, deps=(gdeps + [self.xnb_tok[k]]) if tb == 0 else (), inc=False)
            last = None
            for tb in range(4):
                def mmbt(t, tb=tb, bk=banks[tb], do=do):
                    return t.matmul(self.bank(bk), self.BTW[:, do * 128:(do + 1) * 128],
                                    AXW[:, tb * 512:(tb + 1) * 512], start=False, stop=True)
                last = P.op("tensor", mmbt, deps=[axw_toks[tb]], inc=(tb == 3))
            self.wsub_busy[sub] = last
            if do + 4 < NCH:
                wtok[do + 4] = self.load_wcf(self.wft, (do + 4) * 128, (do + 4) % 8)
            for tb in range(4):
                cons = res_writer(do, tb, banks[tb], [last])
                self.rel([banks[tb]], cons)
        all_done = [tmax(*self.cur_x_writes.values())]
        self.finish_stage(xsrc, xdst, [wb_all], [], all_done)

    def emit_attn_b(self, xsrc, xdst):
        P = self.P
        self.begin_stage()
        self.emit_stats_cast(xsrc)
        self.proj_feature(self.wqkv, 0, self.feat_writer(
            lambda do, tb, ev: (self.b_q[tb * 4:(tb + 1) * 4, :, do, :].rearrange("c p f -> p c f"),
                                ev.rearrange("p (c f) -> p c f", f=128)), "bqkv_q", "b_q"))
        self.proj_feature(self.wqkv, D, self.feat_writer(
            lambda do, tb, ev: (self.b_k[tb * 4:(tb + 1) * 4, :, do, :].rearrange("c p f -> p c f"),
                                ev.rearrange("p (c f) -> p c f", f=128)), "bqkv_k", "b_k"))
        BVB = self.arf(4096, 2048)
        bv_tok = P.dma("sync", CH_R2, lambda s: s.dma_start(
            out=BVB, in_=self.row_bqv.broadcast_to([128, D])), deps=self.xnb_tok + [self.stat_busy[0]])
        self.proj_token(self.wqkv, 2 * D, D, self.tok_writer(
            lambda tc, dob, ev: (self.b_v[tc, :, dob * 512:(dob + 1) * 512], ev), BVB, bv_tok, "b_v"))
        bq_all = tmax(*self.cur_scr_writes["b_q"])
        bk_all = tmax(*self.cur_scr_writes["b_k"])
        bv_all = tmax(*self.cur_scr_writes["b_v"])
        phase_gate = tmax(bq_all, bk_all, bv_all, self.xnb_all, *self.stat_busy, bv_tok)

        PREFB = lambda sl: self.XB[:, sl * 8192:(sl + 1) * 8192]
        S_sb = self.XBF[:, 8192:10240]
        SSB = self.XB[:, 20480:22528]
        QDB = self.XB[:, 22528:24576]
        STB = self.XB[:, 24576:26624]
        KDB = self.XB[:, 26624:28672]
        VC = lambda sl: self.XB[:, 28672 + sl * 2048: 28672 + (sl + 1) * 2048]
        QC = lambda sl: self.arb(0 + sl * 2048, 2048)
        KC = lambda sl: self.arb(4096 + sl * 2048, 2048)

        z0 = P.op("vector", lambda v: v.memset(S_sb[:, :], 0.0), deps=[phase_gate])
        z1 = P.op("vector", lambda v: v.memset(SSB[:, :], 0.0), deps=[phase_gate])
        state_tok = {h: [z0, z1] for h in range(H)}
        qc_busy = [None, None]
        kc_busy = [None, None]
        vc_busy = [None, None]
        stb_gate = {}
        qdb_gate = {}
        kdb_gate = {}
        prefb_gate = [None, None]
        b3_reads = []
        res_writer = self.residual_writer(xsrc, xdst, "bout")
        o_evac = {}
        for c in range(NCH):
            slot = c % 2
            tqc = P.dma("sync", CH_R0 + slot, lambda s, c=c, slot=slot: s.dma_start(
                out=QC(slot)[:, :].rearrange("p (h f) -> p h f", f=128),
                in_=self.b_q[c]), deps=[qc_busy[slot], phase_gate])
            tkc = P.dma("sync", CH_R2 + slot, lambda s, c=c, slot=slot: s.dma_start(
                out=KC(slot)[:, :].rearrange("p (h f) -> p h f", f=128),
                in_=self.b_k[c]), deps=[kc_busy[slot], phase_gate])
            tvc = P.dma("sync", CH_R4 + slot, lambda s, c=c, slot=slot: s.dma_start(
                out=VC(slot)[:, :], in_=self.b_v[c]), deps=[vc_busy[slot], phase_gate])
            b3_reads += [tqc, tkc, tvc]
            chunk_vc_uses = []
            for h in range(H):
                hp = h % 2
                st_b, o_b, u_b = hp, 2 + hp, 4 + hp
                hs = slice(h * 128, (h + 1) * 128)
                m_st = P.op("tensor", lambda t, bk=st_b, hs=hs, slot=slot: t.matmul(
                    self.bank(bk)[:, 0:128], KC(slot)[:, hs], QC(slot)[:, hs],
                    start=True, stop=True), deps=self.acq([st_b], [tqc, tkc]))
                v_dec = P.op("vector", lambda v, bk=st_b, hs=hs: v.tensor_mul(
                    STB[:, hs], self.bank(bk)[:, 0:128], self.INTRA[:, hs]),
                    deps=[m_st, phase_gate if c == 0 else None] + (stb_gate.get(h) or []))
                self.rel([st_b], [v_dec])
                kd_bank = 6 + hp
                kd_ap = self.PDB[:, (kd_bank - 6) * 1024:(kd_bank - 6) * 1024 + 128]
                m_tr = P.op("tensor", lambda t, kd_ap=kd_ap, hs=hs, slot=slot: t.transpose(
                    kd_ap, KC(slot)[:, hs], self.EYEB[:, :]), deps=self.acq([kd_bank], [tkc]))
                s_kd = P.op("scalar", lambda s, kd_ap=kd_ap, hs=hs, h=h: s.mul(
                    KDB[:, hs], kd_ap, self.SLP[:, 48 + h:49 + h]),
                    deps=[m_tr, phase_gate if c == 0 else None] + (kdb_gate.get(h) or []))
                self.rel([kd_bank], [s_kd])
                v_qd = P.op("vector", lambda v, hs=hs, slot=slot: v.tensor_mul(
                    QDB[:, hs], QC(slot)[:, hs], self.QDECF[:, hs]),
                    deps=[tqc, phase_gate if c == 0 else None] + (qdb_gate.get(h) or []))
                m_oi = P.op("tensor", lambda t, bk=o_b, hs=hs, slot=slot: t.matmul(
                    self.bank(bk)[:, 0:128], VC(slot)[:, hs], STB[:, hs],
                    start=True, stop=False), deps=self.acq([o_b], [v_dec, tvc]))
                m_oe = P.op("tensor", lambda t, bk=o_b, hs=hs: t.matmul(
                    self.bank(bk)[:, 0:128], SSB[:, hs], QDB[:, hs],
                    start=False, stop=True), deps=[v_qd] + state_tok[h])
                sl_p = (c // 4) % 2
                s_oe = P.op("scalar", lambda s, bk=o_b, h=h, c=c, sl_p=sl_p: s.copy(
                    PREFB(sl_p)[:, h * 512 + (c % 4) * 128: h * 512 + (c % 4) * 128 + 128],
                    self.bank(bk)[:, 0:128]),
                    deps=[m_oe, prefb_gate[sl_p]])
                self.rel([o_b], [s_oe])
                o_evac.setdefault(c // 4, []).append(s_oe)
                m_up = P.op("tensor", lambda t, bk=u_b, hs=hs, slot=slot: t.matmul(
                    self.bank(bk)[:, 0:128], KDB[:, hs], VC(slot)[:, hs],
                    start=True, stop=True), deps=self.acq([u_b], [s_kd, tvc]))
                v_st = P.op("vector", lambda v, bk=u_b, hs=hs, h=h: v.scalar_tensor_tensor(
                    S_sb[:, hs], S_sb[:, hs], self.SLP[:, 32 + h:33 + h],
                    self.bank(bk)[:, 0:128], ALU.mult, ALU.add), deps=[m_up] + state_tok[h])
                s_cast = P.op("scalar", lambda s, hs=hs: s.copy(SSB[:, hs], S_sb[:, hs]),
                              deps=[v_st, m_oe])
                self.rel([u_b], [v_st])
                state_tok[h] = [s_cast]
                stb_gate[h] = [m_oi]
                qdb_gate[h] = [m_oe]
                kdb_gate[h] = [m_up]
                chunk_vc_uses += [m_oi, m_up]
            qc_busy[slot] = tmax(*(qdb_gate[h] for h in range(H)))
            kc_busy[slot] = tmax(*(kdb_gate[h] for h in range(H)))
            vc_busy[slot] = tmax(*chunk_vc_uses)
            if c % 4 == 3:
                tbg = c // 4
                sl_p = tbg % 2
                wtok = {d0: self.load_wcf(self.wout, d0 * 128, d0 % 8) for d0 in range(4)}
                pf_done = o_evac[tbg]
                gates = []
                for do in range(NCH):
                    sub = do % 8
                    bk = do % 4
                    gdeps = self.acq([bk], [wtok[do]]) + pf_done
                    last = None
                    for k in range(NCH):
                        def mm(t, k=k, bk=bk, sub=sub, sl_p=sl_p):
                            return t.matmul(self.bank(bk),
                                            self.wcf(sub)[:, k * 128:(k + 1) * 128],
                                            PREFB(sl_p)[:, k * 512:(k + 1) * 512],
                                            start=(k == 0), stop=(k == NCH - 1))
                        last = P.op("tensor", mm, deps=gdeps if k == 0 else (), inc=(k == NCH - 1))
                    self.wsub_busy[sub] = last
                    gates.append(last)
                    if do + 4 < NCH:
                        wtok[do + 4] = self.load_wcf(self.wout, (do + 4) * 128, (do + 4) % 8)
                    cons = res_writer(do, tbg, bk, [last])
                    self.rel([bk], cons)
                prefb_gate[sl_p] = tmax(*gates)
        all_done = [tmax(*self.cur_x_writes.values())]
        self.finish_stage(xsrc, xdst, b3_reads, ["b_q", "b_k", "b_v"], all_done)


def build_program(n_stages=4):
    kb = KB(n_stages)
    with kb.es:
        kb.emit_all()
        block = kb.es.enter_context(kb.nc.Block())
        kb.P.build(block)
    return kb.nc



def _host_consts():
    import ml_dtypes
    i = np.arange(128, dtype=np.float32)
    c = {}
    c["c_eye_b"] = np.eye(128, dtype=np.float32)
    c["c_eye_f"] = np.eye(128, dtype=np.float32)
    dif = i[:, None] - i[None, :]
    c["c_dif"] = np.where(dif <= 0, dif, -1.0).astype(np.float32)
    c["c_mask"] = (i[:, None] <= i[None, :]).astype(np.float32)
    c["c_rowidx"] = np.ascontiguousarray(np.broadcast_to(i[None, :], (128, 128)))
    c["c_col128m"] = np.ascontiguousarray((128.0 - i)[:, None])
    c["c_onescol"] = np.ones((128, 1), dtype=np.float32)
    c["c_allones"] = np.ones((128, 128), dtype=np.float32)
    for k in ("c_eye_b", "c_onescol", "c_allones"):
        c[k] = c[k].astype(ml_dtypes.bfloat16)
    return c


def _host_prep(inputs):
    x = np.asarray(inputs["L0_attn_a_x"], dtype=np.float32)
    common = {
        "wq": np.asarray(inputs["L0_attn_a_Wq"], dtype=np.float32),
        "wk": np.asarray(inputs["L0_attn_a_Wk"], dtype=np.float32),
        "wv": np.asarray(inputs["L0_attn_a_Wv"], dtype=np.float32),
        "wo": np.asarray(inputs["L0_attn_a_Wo"], dtype=np.float32),
        "wqkv": np.asarray(inputs["L0_attn_b_Wqkv"], dtype=np.float32),
        "wout": np.asarray(inputs["L0_attn_b_Wout"], dtype=np.float32),
        "wft": np.ascontiguousarray(np.asarray(inputs["L0_ffn_weight"], dtype=np.float32).T),
        "slopes": np.asarray(inputs["L0_attn_b_slopes"], dtype=np.float32).reshape(1, H),
        "row_bv": np.asarray(inputs["L0_attn_a_bv"], dtype=np.float32).reshape(1, D),
        "row_bqv": np.ascontiguousarray(
            np.asarray(inputs["L0_attn_b_bqkv"], dtype=np.float32)[2 * D:].reshape(1, D)),
    }
    rwt = np.asarray(inputs["L0_ffn_router_w"], dtype=np.float32).T
    common["rwt"] = np.ascontiguousarray(rwt.reshape(NCH, 128, 8).transpose(1, 0, 2).reshape(128, 128))
    at = np.asarray(inputs["L0_ffn_lora_A"], dtype=np.float32)
    atf = at.transpose(2, 0, 1).reshape(D, 128)
    common["atw"] = np.ascontiguousarray(atf.reshape(NCH, 128, 128).transpose(1, 0, 2).reshape(128, NCH * 128))
    bt = np.asarray(inputs["L0_ffn_lora_B"], dtype=np.float32)
    common["btw"] = np.ascontiguousarray(bt.transpose(0, 2, 1).reshape(128, D))
    for name, arr in (("bq", inputs["L0_attn_a_bq"]), ("bk", inputs["L0_attn_a_bk"]),
                      ("bo", inputs["L0_attn_a_bo"]), ("bout", inputs["L0_attn_b_bout"])):
        common[f"pb_{name}"] = np.ascontiguousarray(
            np.asarray(arr, dtype=np.float32).reshape(NCH, 128).T)
    bqkv = np.asarray(inputs["L0_attn_b_bqkv"], dtype=np.float32)
    common["pb_bqkv_q"] = np.ascontiguousarray(bqkv[0:D].reshape(NCH, 128).T)
    common["pb_bqkv_k"] = np.ascontiguousarray(bqkv[D:2 * D].reshape(NCH, 128).T)
    common.update(_host_consts())
    in_maps = []
    for core in range(NCORE):
        xm = np.ascontiguousarray(x[core * NB:(core + 1) * NB].transpose(0, 2, 1))
        m = dict(common)
        m["xin"] = xm
        in_maps.append(m)
    return in_maps


_BUILD_CACHE = {}


def kernel(**inputs):
    n_stages = 4
    if n_stages not in _BUILD_CACHE:
        _BUILD_CACHE[n_stages] = build_program(n_stages)
    nc = _BUILD_CACHE[n_stages]
    in_maps = _host_prep(inputs)
    res = run_bass_kernel_spmd(nc, in_maps, core_ids=list(range(NCORE)))
    outs = []
    for core in range(NCORE):
        y = np.asarray(res.results[core]["y"])
        outs.append(y.transpose(0, 2, 1))
    return np.ascontiguousarray(np.concatenate(outs, axis=0))
